# version 23
# baseline (speedup 1.0000x reference)
import contextlib

import numpy as np

import concourse.bass as bass
import concourse.mybir as mybir
from concourse.bass_utils import run_bass_kernel_spmd

F32 = mybir.dt.float32
BF16 = mybir.dt.bfloat16
I32 = mybir.dt.int32
I8 = mybir.dt.int8
AF = mybir.ActivationFunctionType
OP = mybir.AluOpType

LN_EPS = 1e-5
WKV_EPS = 1e-8
QS = 1.0 / 127.0
T = 512
ENGS = ("T", "V", "S", "G", "D")
_ENG_ATTR = {"T": "tensor", "V": "vector", "S": "scalar", "G": "gpsimd", "D": "sync"}
_COMPUTE = ("T", "V", "S")
_DTSZ = {F32: 4, BF16: 2, I32: 4, I8: 1}


class _MockIns:
    def then_inc(self, *a, **k):
        return self


class _MockEng:
    def __getattr__(self, name):
        return lambda *a, **k: _MockIns()


class _Em:

    PARANOID = False

    def __init__(self, nc, es, block):
        self.nc = nc
        self.block = block
        self.sem = {e: es.enter_context(nc.semaphore("sem" + e)) for e in ENGS}
        self.cnt = {e: 0 for e in ENGS}
        self.cnt2 = {e: 0 for e in ENGS}
        self._waited = {}
        self._thunks = {e: [] for e in ENGS}

    def step(self, eng, waits=(), body=None):
        if _Em.PARANOID:
            waits = [(o, self.cnt[o]) for o in ENGS if o != eng]
        fwaits = []
        for we, wv in waits:
            if wv is None or wv <= 0:
                continue
            if we == eng and eng in _COMPUTE:
                continue
            key = (eng, we)
            if self._waited.get(key, 0) >= wv:
                continue
            fwaits.append((we, wv))
            self._waited[key] = wv
        if body is not None:
            def mark1(ins, n=1):
                self.cnt[eng] += n
                return self.cnt[eng]
            body(_MockEng(), mark1)
        self._thunks[eng].append((fwaits, body))
        return self.cnt[eng]

    def finalize(self):
        for eng in ENGS:
            thunks = self._thunks[eng]

            def f(h, eng=eng, thunks=thunks):
                def mark2(ins, n=1):
                    ins.then_inc(self.sem[eng], n)
                    self.cnt2[eng] += n
                    return self.cnt2[eng]
                for fwaits, body in thunks:
                    for we, wv in fwaits:
                        h.wait_ge(self.sem[we], wv)
                    if body is not None:
                        body(h, mark2)
            getattr(self.block, _ENG_ATTR[eng])(f)
        for e in ENGS:
            assert self.cnt2[e] == self.cnt[e], (e, self.cnt2[e], self.cnt[e])

    def snap(self):
        return dict(self.cnt)


class _Arena:

    def __init__(self, ap, nbytes):
        self.ap = ap
        self.nbytes = nbytes
        self.off = 0

    def alloc(self, shape, dt, parts=128):
        n = 1
        for s in shape[1:]:
            n *= s
        sz = n * _DTSZ[dt]
        off = (self.off + 63) & ~63
        assert off + sz <= self.nbytes, (off, sz, self.nbytes)
        self.off = off + sz
        v = self.ap[0:parts, off:off + sz].bitcast(dt)
        if len(shape) == 2:
            return v
        if len(shape) == 3:
            return v.rearrange("p (c t) -> p c t", c=shape[1])
        assert len(shape) == 4
        return v.rearrange("p (a b t) -> p a b t", a=shape[1], b=shape[2])

    def mark_(self):
        return self.off

    def reset(self, off=0):
        self.off = off


def _build(NG=8, phases=("A", "F", "C"), debug=False):
    nc = bass.Bass("TRN2", target_bir_lowering=False)
    IN, OUT = "ExternalInput", "ExternalOutput"
    SCR = OUT if debug else "Internal"

    def dram(name, shape, dt, kind):
        return nc.dram_tensor(name, shape, dt, kind=kind).ap()

    x_h = dram("x", [128, 8, 4096], F32, IN)
    wA_names = ("wr", "ww", "wk", "wv", "wo")
    wA_h = {n: dram(n, [128, 8, 1024], F32, IN) for n in wA_names}
    wrec_h = dram("wrec", [128, 8, 1024], F32, IN)
    wkey_h = dram("wkey", [128, 8, 4096], F32, IN)
    wval_h = dram("wval", [128, 32, 1024], F32, IN)
    wg_h = dram("wg", [128, 8, 3072], I8, IN)
    wu_h = dram("wu", [128, 8, 3072], I8, IN)
    wd_h = dram("wd", [128, 24, 1024], I8, IN)
    g1_h = dram("g1", [128, 8], F32, IN)
    b1_h = dram("b1", [128, 8], F32, IN)
    g2_h = dram("g2", [128, 8], F32, IN)
    b2_h = dram("b2", [128, 8], F32, IN)
    td_h = dram("td", [128, 8], F32, IN)
    out_h = dram("out", [128, 8, 4096], F32, OUT)
    xn_sp = dram("xn_sp", [128, 8, 4096], BF16, SCR)
    xn2_sp = dram("xn2_sp", [128, 8, 4096], BF16, SCR)
    y_sp = dram("y_sp", [128, 8, 4096], F32, SCR)
    acc_sp = dram("acc_sp", [128, 8, 4096], F32, SCR)
    cv_sp = dram("cv_sp", [128, 8, 4096], F32, SCR)

    with contextlib.ExitStack() as es0:
        ps = es0.enter_context(nc.psum_tensor("ps", [128, 4096], F32))
        block = es0.enter_context(nc.Block())
        em = _Em(nc, es0, block)

        def bank(b):
            return ps[:, b * 512:(b + 1) * 512]

        S1a = ps[0:1, 6 * 512:6 * 512 + 512]
        S2a = ps[0:1, 7 * 512:7 * 512 + 512]
        S1b = ps[0:1, 2 * 512:2 * 512 + 512]
        S2b = ps[0:1, 3 * 512:3 * 512 + 512]

        def psb(name, shape, dt):
            return es0.enter_context(nc.sbuf_tensor(name, shape, dt))

        g1_t = psb("g1_t", [128, 8], F32)
        b1_t = psb("b1_t", [128, 8], F32)
        g2_t = psb("g2_t", [128, 8], F32)
        b2_t = psb("b2_t", [128, 8], F32)
        td_t = psb("td_t", [128, 8], F32)
        dec_t = psb("dec_t", [128, 8], F32)
        cn_t = psb("cn_t", [128, 8], F32)
        cd_t = psb("cd_t", [128, 8], F32)
        ones_bf = psb("ones_bf", [128, 1], BF16)
        ones_rf = psb("ones_rf", [1, 128], F32)

        AR_BYTES = 190 * 1024
        ar = _Arena(es0.enter_context(
            nc.sbuf_tensor("arena", [128, AR_BYTES], I8)).ap(), AR_BYTES)

        m = {}

        def d_init(h, mark):
            for ap_h, t in ((g1_h, g1_t), (b1_h, b1_t), (g2_h, g2_t),
                            (b2_h, b2_t), (td_h, td_t)):
                mark(h.dma_start(out=t[:], in_=ap_h), 16)
        m["d_vec"] = em.step("D", body=d_init)

        def v_init(h, mark):
            h.memset(ones_bf[:], 1.0)
            h.memset(ones_rf[:], 1.0)
            h.memset(cn_t[:], 0.0)
            mark(h.memset(cd_t[:], 0.0))
        m["v_init"] = em.step("V", body=v_init)

        def s_init(h, mark):
            mark(h.activation(out=dec_t[:], in_=td_t[:], func=AF.Exp))
        m["s_dec"] = em.step("S", waits=[("D", m["d_vec"])], body=s_init)

        wA = {n: ar.alloc([128, 8, 1024], BF16) for n in wA_names}
        ffn_base = 0
        x_t = ar.alloc([128, 8, T], F32)
        xb2 = ar.alloc([128, 2, T], BF16)
        sqb = ar.alloc([128, 2, T], BF16)
        rows = ar.alloc([1, 7, T], F32, parts=1)
        bc_t = ar.alloc([128, 6, T], F32)
        xn_t = ar.alloc([128, 8, T], F32)
        xn1_bf = ar.alloc([128, 8, T], BF16)
        t512 = ar.alloc([128, T], F32)
        ew_t = ar.alloc([128, 2, T], F32)
        e_t = ar.alloc([128, 2, T], F32)
        km_t = ar.alloc([128, 2, T], F32)
        num_t = ar.alloc([128, 2, T], F32)
        den_t = ar.alloc([128, T], F32)
        dd_t = ar.alloc([128, 2, T], F32)
        rn_t = ar.alloc([128, 2, T], F32)
        q_t = ar.alloc([128, T], F32)
        rw_bf = ar.alloc([128, 8, T], BF16)
        x2s_bf = ar.alloc([128, 2, T], BF16)
        amark = ar.mark_()

        def rowv(i):
            return rows[:, i, :]

        def bcv(i):
            return bc_t[:, i, :].rearrange("p (o t) -> p o t", o=1) \
                .broadcast_to([128, 8, T])

        if "A" in phases:
            def g_wA(h, mark):
                for n in wA_names:
                    mark(h.dma_start(out=wA[n][:], in_=wA_h[n]), 16)
            m["g_wA"] = em.step("G", body=g_wA)

            worder = ("wr", "ww", "wk", "wv")
            names = ("zrmm", "zwmm", "kmm", "vmm")

            def emit_sqF(g, dc):
                def v_st(h, mark, dc=dc, g=g):
                    h.tensor_copy(out=xb2[:, dc % 2, :], in_=x_t[:, dc, :])
                    mark(h.tensor_tensor(out=sqb[:, dc % 2, :],
                                         in0=xb2[:, dc % 2, :],
                                         in1=xb2[:, dc % 2, :], op=OP.mult))
                w = [("D", m[f"xld{g}_{dc}"])]
                if dc >= 2:
                    w.append(("T", m[f"st1mm{g}_{dc-2}"]))
                elif g == 1:
                    w.append(("T", m[f"st1mm0_{6+dc}"]))
                elif g > 1:
                    w.append(("T", m[f"st2mm{g-2}_{6+dc}"]))
                m[f"sqF{g}_{dc}"] = em.step("V", waits=w, body=v_st)

            def emit_st1(g, dc):
                def t_st(h, mark, dc=dc):
                    h.matmul(S1a, ones_bf[:], xb2[:, dc % 2, :],
                             start=(dc == 0), stop=(dc == 7))
                    mark(h.matmul(S2a, ones_bf[:], sqb[:, dc % 2, :],
                                  start=(dc == 0), stop=(dc == 7)))
                w = [("V", m[f"sqF{g}_{dc}"]), ("G", m["g_wA"]),
                     ("V", m["v_init"])]
                if dc == 0 and g > 0:
                    w += [("V", m[f"r1aF{g-1}"]), ("V", m[f"u{g-1}_7"])]
                m[f"st1mm{g}_{dc}"] = em.step("T", waits=w, body=t_st)

            def emit_FE_rows(g):
                def v_r1(h, mark):
                    h.tensor_scalar_mul(out=rowv(0), in0=S1a, scalar1=1.0 / 1024)
                    h.tensor_scalar(out=rowv(1), in0=S2a, scalar1=1.0 / 1024,
                                    scalar2=LN_EPS, op0=OP.mult, op1=OP.add)
                    h.tensor_tensor(out=rowv(2), in0=rowv(0), in1=rowv(0),
                                    op=OP.mult)
                    mark(h.tensor_tensor(out=rowv(3), in0=rowv(1), in1=rowv(2),
                                         op=OP.subtract))
                w = [("T", m[f"st1mm{g}_7"])]
                if g == 1:
                    w.append(("T", m["bcFmm0_3"]))
                elif g > 1:
                    w.append(("T", m[f"bc2mm{g-2}_1"]))
                m[f"r1aF{g}"] = em.step("V", waits=w, body=v_r1)

                def s_r1(h, mark):
                    h.activation(out=rowv(4), in_=rowv(3), func=AF.Ln)
                    mark(h.activation(out=rowv(5), in_=rowv(4), func=AF.Exp,
                                      scale=-0.5))
                m[f"sr1F{g}"] = em.step("S", waits=[("V", m[f"r1aF{g}"])],
                                        body=s_r1)

                def v_r2(h, mark):
                    h.tensor_tensor(out=rowv(2), in0=rowv(5), in1=rowv(5),
                                    op=OP.mult)
                    mark(h.tensor_scalar(out=rowv(3), in0=rowv(2),
                                         scalar1=-LN_EPS, scalar2=1.0 + LN_EPS,
                                         op0=OP.mult, op1=OP.add))
                m[f"r2F{g}"] = em.step("V", waits=[("S", m[f"sr1F{g}"])],
                                       body=v_r2)

                def s_r2(h, mark):
                    h.activation(out=rowv(4), in_=rowv(3), func=AF.Ln)
                    mark(h.activation(out=rowv(6), in_=rowv(4), func=AF.Exp,
                                      scale=-0.5))
                m[f"sr2F{g}"] = em.step("S", waits=[("V", m[f"r2F{g}"])],
                                        body=s_r2)

                def v_r3(h, mark):
                    h.tensor_tensor(out=rowv(4), in0=rowv(5), in1=rowv(6),
                                    op=OP.mult)
                    h.scalar_tensor_tensor(out=rowv(2), in0=rowv(0),
                                           scalar=-1.0, in1=rowv(4),
                                           op0=OP.mult, op1=OP.mult)
                    mark(h.scalar_tensor_tensor(out=rowv(1), in0=rowv(0),
                                                scalar=-1.0, in1=rowv(5),
                                                op0=OP.mult, op1=OP.mult))
                m[f"r1bF{g}"] = em.step("V", waits=[("S", m[f"sr2F{g}"])],
                                        body=v_r3)

            def emit_FE_bc(g, part):
                srcs = (4, 2, 5, 1)
                for i in (part * 2, part * 2 + 1):
                    sl = srcs[i]
                    bk = 6 + (i % 2)
                    w = [("V", m[f"r1bF{g}"])]
                    if i >= 2:
                        w.append(("S", m[f"cpF{g}_{i-2}"]))
                    m[f"bcFmm{g}_{i}"] = em.step(
                        "T", waits=w, body=lambda h, mark, bk=bk, sl=sl: mark(
                            h.matmul(bank(bk), ones_rf[:], rowv(sl),
                                     start=True, stop=True)))
                    w = [("T", m[f"bcFmm{g}_{i}"])]
                    if i < 2 and g > 0:
                        w.append(("V", m[f"xn1{g-1}_7"]))
                    if i >= 2 and g > 0:
                        w.append(("V", m[f"xn{g-1}_7"]))
                    m[f"cpF{g}_{i}"] = em.step(
                        "S", waits=w, body=lambda h, mark, bk=bk, i=i: mark(
                            h.copy(out=bc_t[:, i, :], in_=bank(bk))))

            def emit_bc2(g):
                m[f"bc2mm{g}_0"] = em.step(
                    "T", waits=[("V", m[f"r1b2L{g}"])],
                    body=lambda h, mark, g=g: mark(
                        h.matmul(bank(4), ones_rf[:], rowv(5),
                                 start=True, stop=True)))
                m[f"bc2mm{g}_1"] = em.step(
                    "T", body=lambda h, mark, g=g: mark(
                        h.matmul(bank(5), ones_rf[:], rowv(1),
                                 start=True, stop=True)))
                w = [("T", m[f"bc2mm{g}_0"])]
                if g > 0:
                    w.append(("G", m[f"x2g{g-1}_7"]))
                m[f"cp2a{g}"] = em.step(
                    "S", waits=w, body=lambda h, mark, g=g: mark(
                        h.copy(out=bc_t[:, 4, :], in_=bank(4))))
                m[f"cp2b{g}"] = em.step(
                    "S", waits=[("T", m[f"bc2mm{g}_1"])],
                    body=lambda h, mark, g=g: mark(
                        h.copy(out=bc_t[:, 5, :], in_=bank(5))))

            def emit_xn1(g, dc):
                def v_x1(h, mark, dc=dc):
                    h.tensor_tensor(out=q_t[:], in0=x_t[:, dc, :],
                                    in1=bc_t[:, 0, :], op=OP.mult)
                    h.tensor_tensor(out=q_t[:], in0=q_t[:],
                                    in1=bc_t[:, 1, :], op=OP.add)
                    mark(h.tensor_scalar(out=xn1_bf[:, dc, :], in0=q_t[:],
                                         scalar1=g1_t[:, dc:dc + 1],
                                         scalar2=b1_t[:, dc:dc + 1],
                                         op0=OP.mult, op1=OP.add))
                m[f"xn1{g}_{dc}"] = em.step(
                    "V", waits=[("S", m[f"cpF{g}_1"])], body=v_x1)

            def d_xld_full(g):
                def d_x(h, mark, g=g):
                    mk = mark(h.dma_start(
                        out=x_t[:], in_=x_h[:, :, g * T:(g + 1) * T]), 16)
                    for dc in range(8):
                        m[f"xld{g}_{dc}"] = mk
                w = []
                if g > 0:
                    w = [("V", m[f"xn{g-1}_7"]), ("V", m[f"xn1{g-1}_7"])]
                em.step("D", waits=w, body=d_x)

            if "F" in phases:
                _sv = ar.mark_()
                ar.reset(ffn_base)
                wgb = ar.alloc([128, 8, 1536], BF16)
                wub = ar.alloc([128, 8, 1536], BF16)
                wdb = ar.alloc([128, 12, 1024], BF16)
                ar.reset(_sv)

                def g_wf(hf):
                    hs = slice(hf * 1536, (hf + 1) * 1536)
                    hcs = slice(hf * 12, (hf + 1) * 12)
                    w = [("T", m[f"vmm{NG-1}_7"])] if hf == 0 else \
                        [("T", m[f"gu0_{NG-1}_11"])]
                    m[f"gwf{hf}a"] = em.step(
                        "G", waits=w, body=lambda h, mark, hs=hs: (
                            mark(h.dma_start(out=wgb[:], in_=wg_h[:, :, hs]),
                                 16),
                            mark(h.dma_start(out=wub[:], in_=wu_h[:, :, hs]),
                                 16))[-1])
                    w = [("T", m[f"tm{NG-1}_7"])] if hf == 0 else \
                        [("T", m[f"dn0_{NG-1}_7"])]
                    m[f"gwf{hf}"] = em.step(
                        "G", waits=w, body=lambda h, mark, hcs=hcs: mark(
                            h.dma_start(out=wdb[:], in_=wd_h[:, hcs, :]), 16))

            d_xld_full(0)
            for dc in range(8):
                emit_sqF(0, dc)
                emit_st1(0, dc)
            emit_FE_rows(0)
            emit_FE_bc(0, 0)
            emit_FE_bc(0, 1)
            for dc in range(8):
                emit_xn1(0, dc)

            def v_xn0(h, mark):
                h.tensor_tensor(out=xn_t[:], in0=x_t[:], in1=bcv(2),
                                op=OP.mult)
                mk = mark(h.tensor_tensor(out=xn_t[:], in0=xn_t[:],
                                          in1=bcv(3), op=OP.add))
                for dc in range(8):
                    m[f"xn0_{dc}"] = mk
            em.step("V", waits=[("S", m["cpF0_3"])], body=v_xn0)

            def g_xnsp0(h, mark):
                mk = mark(h.dma_start(out=xn_sp[:, :, 0:T], in_=xn_t[:]), 16)
                for dc in range(8):
                    m[f"xnsp0_{dc}"] = mk
            em.step("G", waits=[("V", m["xn0_7"])], body=g_xnsp0)
            d_xld_full(1)

            for g in range(NG):
                for ec in range(8):
                    base = 0 if ec % 2 == 0 else 4
                    pr = ec % 2
                    w = [("V", m[f"xn1{g}_7"])]
                    if ec == 0 and g > 0:
                        w.append(("V", m[f"r1cL{g-1}"]))
                    if ec == 1:
                        w.append(("S", m[f"cpF{g}_3"]))
                        if g > 0:
                            w.append(("S", m[f"cp2b{g-1}"]))
                    if ec >= 2:
                        w += [("V", m[f"u{g}_{ec-2}"]),
                              ("S", m[f"rn{g}_{ec-2}"])]
                    def t_proj(h, mark, ec=ec, base=base, g=g):
                        for wi, n in enumerate(worder):
                            ins = None
                            for dc in range(8):
                                ins = h.matmul(bank(base + wi),
                                               wA[n][:, dc,
                                                     ec * 128:(ec + 1) * 128],
                                               xn1_bf[:, dc, :],
                                               start=(dc == 0), stop=(dc == 7))
                            m[f"{names[wi]}{g}_{ec}"] = mark(ins)
                    em.step("T", waits=w, body=t_proj)
                    if ec == 0 and g > 0:
                        emit_bc2(g - 1)

                    w = [("T", m[f"zwmm{g}_{ec}"])]
                    if ec >= 2:
                        w.append(("V", m[f"km{g}_{ec-2}"]))
                    elif g > 0:
                        w.append(("V", m[f"km{g-1}_{ec+6}"]))
                    m[f"ew{g}_{ec}"] = em.step(
                        "S", waits=w, body=lambda h, mark, base=base, pr=pr: mark(
                            h.activation(out=ew_t[:, pr, :], in_=bank(base + 1),
                                         func=AF.Exp)))
                    if ec >= 1:
                        m[f"rcp{g}_{ec-1}"] = em.step(
                            "S", waits=[("V", m[f"vdd{g}_{ec-1}"])],
                            body=lambda h, mark, pr=1 - pr: (
                                h.activation(out=dd_t[:, pr, :],
                                             in_=dd_t[:, pr, :], func=AF.Ln),
                                mark(h.activation(out=dd_t[:, pr, :],
                                                  in_=dd_t[:, pr, :],
                                                  func=AF.Exp,
                                                  scale=-1.0)))[-1])
                    m[f"km{g}_{ec}"] = em.step(
                        "V", waits=[("T", m[f"kmm{g}_{ec}"]),
                                    ("S", m[f"ew{g}_{ec}"])],
                        body=lambda h, mark, base=base, pr=pr: mark(
                            h.tensor_tensor(out=km_t[:, pr, :],
                                            in0=bank(base + 2),
                                            in1=ew_t[:, pr, :],
                                            op=OP.subtract)))
                    w = [("V", m[f"km{g}_{ec}"])]
                    if ec >= 2:
                        w.append(("V", m[f"sce{g}_{ec-2}"]))
                    elif g > 0:
                        w.append(("V", m[f"sce{g-1}_{ec+6}"]))
                    m[f"e{g}_{ec}"] = em.step(
                        "S", waits=w, body=lambda h, mark, pr=pr: mark(
                            h.activation(out=e_t[:, pr, :], in_=km_t[:, pr, :],
                                         func=AF.Exp)))
                    w = [("T", m[f"zrmm{g}_{ec}"])]
                    if ec >= 2:
                        w.append(("V", m[f"vdd{g}_{ec-2}"]))
                    elif g > 0:
                        w.append(("V", m[f"vdd{g-1}_{ec+6}"]))
                    def s_rn(h, mark, base=base, pr=pr):
                        h.activation(out=rn_t[:, pr, :], in_=bank(base + 0),
                                     func=AF.Exp, scale=-1.0)
                        mark(h.add(out=rn_t[:, pr, :], in_=rn_t[:, pr, :],
                                   add=1.0))
                    m[f"rn{g}_{ec}"] = em.step("S", waits=w, body=s_rn)

                    def v_u(h, mark, ec=ec, base=base, g=g, pr=pr):
                        u = h.tensor_tensor(out=km_t[:, pr, :],
                                            in0=e_t[:, pr, :],
                                            in1=bank(base + 3), op=OP.mult)
                        m[f"u{g}_{ec}"] = mark(u)
                    em.step("V", waits=[("S", m[f"e{g}_{ec}"]),
                                        ("T", m[f"vmm{g}_{ec}"]),
                                        ("S", m["s_dec"])], body=v_u)

                    if ec >= 1:
                        m[f"rw{g}_{ec-1}"] = em.step(
                            "V", waits=[("S", m[f"rcp{g}_{ec-1}"])],
                            body=lambda h, mark, ec=ec, pr=1 - pr: mark(
                                h.tensor_tensor(out=rw_bf[:, ec - 1, :],
                                                in0=num_t[:, pr, :],
                                                in1=dd_t[:, pr, :],
                                                op=OP.mult)))

                    def v_scan(h, mark, ec=ec, g=g, pr=pr):
                        dcol = dec_t[:, ec:ec + 1].broadcast_to([128, T])
                        h.tensor_tensor_scan(out=num_t[:, pr, :], data0=dcol,
                                             data1=km_t[:, pr, :],
                                             initial=cn_t[:, ec:ec + 1],
                                             op0=OP.mult, op1=OP.add)
                        h.tensor_tensor_scan(out=den_t[:], data0=dcol,
                                             data1=e_t[:, pr, :],
                                             initial=cd_t[:, ec:ec + 1],
                                             op0=OP.mult, op1=OP.add)
                        h.tensor_copy(out=cn_t[:, ec:ec + 1],
                                      in_=num_t[:, pr, T - 1:T])
                        h.engine_nop()
                        mark(h.tensor_copy(out=cd_t[:, ec:ec + 1],
                                           in_=den_t[:, T - 1:T]))
                    w = []
                    if ec >= 2:
                        w.append(("S", m[f"rcp{g}_{ec-2}"]))
                    elif g > 0:
                        w.append(("S", m[f"rcp{g-1}_{ec+6}"]))
                    m[f"sce{g}_{ec}"] = em.step("V", waits=w, body=v_scan)

                    m[f"vdd{g}_{ec}"] = em.step(
                        "V", waits=[("S", m[f"rn{g}_{ec}"])],
                        body=lambda h, mark, pr=pr: mark(
                            h.scalar_tensor_tensor(out=dd_t[:, pr, :],
                                                   in0=den_t[:],
                                                   scalar=WKV_EPS,
                                                   in1=rn_t[:, pr, :],
                                                   op0=OP.add, op1=OP.mult)))

                    if g > 0:
                        def g_x2(h, mark, dc=ec, g=g):
                            h.tensor_tensor(out=t512[:],
                                            in0=xn_t[:, dc, :],
                                            in1=bc_t[:, 4, :], op=OP.mult)
                            mark(h.tensor_tensor(out=t512[:], in0=t512[:],
                                                 in1=bc_t[:, 5, :], op=OP.add))
                        w = [("S", m[f"cp2b{g-1}"]), ("S", m[f"cp2a{g-1}"])]
                        if ec >= 1:
                            w.append(("V", m[f"x2v{g-1}_{ec-1}"]))
                        elif g > 1:
                            w.append(("V", m[f"x2v{g-2}_7"]))
                        m[f"x2g{g-1}_{ec}"] = em.step("G", waits=w, body=g_x2)

                        def v_x2v(h, mark, dc=ec):
                            mark(h.tensor_scalar(out=x2s_bf[:, dc % 2, :],
                                                 in0=t512[:],
                                                 scalar1=g2_t[:, dc:dc + 1],
                                                 scalar2=b2_t[:, dc:dc + 1],
                                                 op0=OP.mult, op1=OP.add))
                        w = [("G", m[f"x2g{g-1}_{ec}"])]
                        if ec >= 2:
                            w.append(("D", m[f"xsp2{g-1}_{ec-2}"]))
                        elif g > 1:
                            w.append(("D", m[f"xsp2{g-2}_{6+ec}"]))
                        m[f"x2v{g-1}_{ec}"] = em.step("V", waits=w, body=v_x2v)
                        m[f"xsp2{g-1}_{ec}"] = em.step(
                            "D", waits=[("V", m[f"x2v{g-1}_{ec}"])],
                            body=lambda h, mark, g=g, dc=ec: mark(
                                h.dma_start(
                                    out=xn2_sp[:, dc, (g - 1) * T:g * T],
                                    in_=x2s_bf[:, dc % 2, :]), 16))

                        def v_xn(h, mark, dc=ec, g=g):
                            h.tensor_tensor(out=xn_t[:, dc, :],
                                            in0=x_t[:, dc, :],
                                            in1=bc_t[:, 2, :], op=OP.mult)
                            mark(h.tensor_tensor(out=xn_t[:, dc, :],
                                                 in0=xn_t[:, dc, :],
                                                 in1=bc_t[:, 3, :], op=OP.add))
                        w = [("S", m[f"cpF{g}_3"]), ("G", m[f"x2g{g-1}_{ec}"]),
                             ("D", m[f"ysp{g-1}"])]
                        m[f"xn{g}_{ec}"] = em.step("V", waits=w, body=v_xn)
                        m[f"xnsp{g}_{ec}"] = em.step(
                            "G", waits=[("V", m[f"xn{g}_{ec}"])],
                            body=lambda h, mark, g=g, dc=ec: mark(
                                h.dma_start(
                                    out=xn_sp[:, dc, g * T:(g + 1) * T],
                                    in_=xn_t[:, dc, :]), 16))
                        if g + 1 < NG:
                            m[f"xld{g+1}_{ec}"] = em.step(
                                "D", waits=[("V", m[f"xn{g}_{ec}"]),
                                            ("V", m[f"xn1{g}_7"])],
                                body=lambda h, mark, g=g, dc=ec: mark(
                                    h.dma_start(
                                        out=x_t[:, dc, :],
                                        in_=x_h[:, dc,
                                                (g + 1) * T:(g + 2) * T]), 16))

                def s_rcp7(h, mark):
                    h.activation(out=dd_t[:, 1, :], in_=dd_t[:, 1, :],
                                 func=AF.Ln)
                    mark(h.activation(out=dd_t[:, 1, :], in_=dd_t[:, 1, :],
                                      func=AF.Exp, scale=-1.0))
                m[f"rcp{g}_7"] = em.step(
                    "S", waits=[("V", m[f"vdd{g}_7"])], body=s_rcp7)

                if g + 1 < NG:
                    emit_sqF(g + 1, 0)
                    emit_sqF(g + 1, 1)
                m[f"rw{g}_7"] = em.step(
                    "V", waits=[("S", m[f"rcp{g}_7"])],
                    body=lambda h, mark: mark(
                        h.tensor_tensor(out=rw_bf[:, 7, :],
                                        in0=num_t[:, 1, :],
                                        in1=dd_t[:, 1, :], op=OP.mult)))
                if g + 1 < NG:
                    emit_st1(g + 1, 0)
                    emit_st1(g + 1, 1)
                    for dc in range(2, 8):
                        emit_sqF(g + 1, dc)
                        emit_st1(g + 1, dc)
                    emit_FE_rows(g + 1)

                def emit_sqL(g, dc):
                    def s_st2(h, mark, dc=dc):
                        h.activation(out=xb2[:, dc % 2, :], in_=xn_t[:, dc, :],
                                     func=AF.Copy)
                        mark(h.activation(out=sqb[:, dc % 2, :],
                                          in_=xn_t[:, dc, :], func=AF.Square))
                    w = [("V", m[f"y{g}_{dc}"])]
                    if dc >= 2:
                        w.append(("T", m[f"st2mm{g}_{dc-2}"]))
                    else:
                        w.append(("T", m[f"st1mm{g+1}_{6+dc}"] if g + 1 < NG
                                  else m[f"st1mm{g}_{6+dc}"]))
                    m[f"sqL{g}_{dc}"] = em.step("S", waits=w, body=s_st2)

                def emit_st2(g, dc):
                    def t_st2(h, mark, dc=dc):
                        h.matmul(S1b, ones_bf[:], xb2[:, dc % 2, :],
                                 start=(dc == 0), stop=(dc == 7))
                        mark(h.matmul(S2b, ones_bf[:], sqb[:, dc % 2, :],
                                      start=(dc == 0), stop=(dc == 7)))
                    w = [("S", m[f"sqL{g}_{dc}"])]
                    if dc == 0 and g > 0:
                        w.append(("V", m[f"r1cL{g-1}"]))
                    m[f"st2mm{g}_{dc}"] = em.step("T", waits=w, body=t_st2)

                for oc in range(8):
                    w = [("V", m[f"rw{g}_7"])]
                    if oc >= 2:
                        w.append(("V", m[f"y{g}_{oc-2}"]))
                    def t_wo(h, mark, oc=oc, g=g):
                        ins = None
                        for dc in range(8):
                            ins = h.matmul(bank(oc % 2),
                                           wA["wo"][:, dc,
                                                    oc * 128:(oc + 1) * 128],
                                           rw_bf[:, dc, :],
                                           start=(dc == 0), stop=(dc == 7))
                        m[f"tm{g}_{oc}"] = mark(ins)
                    em.step("T", waits=w, body=t_wo)
                    m[f"y{g}_{oc}"] = em.step(
                        "V", waits=[("T", m[f"tm{g}_{oc}"]),
                                    ("G", m[f"xnsp{g}_{oc}"])],
                        body=lambda h, mark, oc=oc: mark(
                            h.tensor_tensor(out=xn_t[:, oc, :],
                                            in0=bank(oc % 2),
                                            in1=xn_t[:, oc, :], op=OP.add)))
                    if oc == 1 and g + 1 < NG:
                        emit_FE_bc(g + 1, 0)
                    if oc >= 4:
                        emit_sqL(g, oc - 4)
                    if oc >= 2 and g + 1 < NG:
                        emit_xn1(g + 1, oc - 2)
                    if oc == 3 and g + 1 < NG:
                        emit_FE_bc(g + 1, 1)
                    if oc >= 4:
                        emit_st2(g, oc - 4)

                for dc in range(4, 8):
                    emit_sqL(g, dc)
                    if g + 1 < NG and dc < 6:
                        emit_xn1(g + 1, dc + 2)
                    emit_st2(g, dc)

                m[f"ysp{g}"] = em.step(
                    "D", waits=[("V", m[f"y{g}_7"])],
                    body=lambda h, mark, g=g: mark(
                        h.dma_start(out=y_sp[:, :, g * T:(g + 1) * T],
                                    in_=xn_t[:]), 16))
                if g == NG - 1 and "F" in phases:
                    g_wf(0)

                def v_r1b(h, mark):
                    h.tensor_scalar_mul(out=rowv(0), in0=S1b, scalar1=1.0 / 1024)
                    h.tensor_scalar(out=rowv(1), in0=S2b, scalar1=1.0 / 1024,
                                    scalar2=LN_EPS, op0=OP.mult, op1=OP.add)
                    h.tensor_tensor(out=rowv(2), in0=rowv(0), in1=rowv(0),
                                    op=OP.mult)
                    mark(h.tensor_tensor(out=rowv(3), in0=rowv(1), in1=rowv(2),
                                         op=OP.subtract))
                w = [("T", m[f"st2mm{g}_7"])]
                if g + 1 < NG:
                    w.append(("T", m[f"bcFmm{g+1}_3"]))
                m[f"r1cL{g}"] = em.step("V", waits=w, body=v_r1b)

                def s_r3(h, mark):
                    h.activation(out=rowv(4), in_=rowv(3), func=AF.Ln)
                    mark(h.activation(out=rowv(5), in_=rowv(4), func=AF.Exp,
                                      scale=-0.5))
                m[f"sr3L{g}"] = em.step("S", waits=[("V", m[f"r1cL{g}"])],
                                        body=s_r3)

                def v_r4(h, mark):
                    mark(h.scalar_tensor_tensor(out=rowv(1), in0=rowv(0),
                                                scalar=-1.0, in1=rowv(5),
                                                op0=OP.mult, op1=OP.mult))
                m[f"r1b2L{g}"] = em.step("V", waits=[("S", m[f"sr3L{g}"])],
                                         body=v_r4)

                if g == NG - 1:
                    emit_bc2(g)

            g = NG - 1
            for dc in range(8):
                def g_x2e(h, mark, dc=dc, g=g):
                    h.tensor_tensor(out=t512[:], in0=xn_t[:, dc, :],
                                    in1=bc_t[:, 4, :], op=OP.mult)
                    mark(h.tensor_tensor(out=t512[:], in0=t512[:],
                                         in1=bc_t[:, 5, :], op=OP.add))
                w = [("S", m[f"cp2b{g}"]), ("S", m[f"cp2a{g}"])]
                if dc >= 1:
                    w.append(("V", m[f"x2v{g}_{dc-1}"]))
                else:
                    w.append(("V", m[f"x2v{g-1}_7"]))
                m[f"x2g{g}_{dc}"] = em.step("V", waits=w, body=g_x2e)

                def v_x2v(h, mark, dc=dc):
                    mark(h.tensor_scalar(out=x2s_bf[:, dc % 2, :],
                                         in0=t512[:],
                                         scalar1=g2_t[:, dc:dc + 1],
                                         scalar2=b2_t[:, dc:dc + 1],
                                         op0=OP.mult, op1=OP.add))
                w = [("G", m[f"x2g{g}_{dc}"])]
                if dc >= 2:
                    w.append(("D", m[f"xsp2{g}_{dc-2}"]))
                else:
                    w.append(("D", m[f"xsp2{g-1}_{6+dc}"]))
                m[f"x2v{g}_{dc}"] = em.step("V", waits=w, body=v_x2v)
                m[f"xsp2{g}_{dc}"] = em.step(
                    "D", waits=[("V", m[f"x2v{g}_{dc}"])],
                    body=lambda h, mark, g=g, dc=dc: mark(
                        h.dma_start(out=xn2_sp[:, dc, g * T:(g + 1) * T],
                                    in_=x2s_bf[:, dc % 2, :]), 16))

        snapA = em.snap()

        if "F" in phases:
            ar.reset(ffn_base)
            ar.alloc([128, 8, 1536], BF16)
            ar.alloc([128, 8, 1536], BF16)
            ar.alloc([128, 12, 1024], BF16)
            xnb_t = ar.alloc([128, 2, 8, T], BF16)
            yF_t = ar.alloc([128, 2, 8, T], F32)
            xF_t = ar.alloc([128, 2, 8, T], F32)
            h_bf = ar.alloc([128, 12, T], BF16)
            sg_t = ar.alloc([128, 2, T], F32)

            first = {"V": True, "S": True}

            def bw(eng, w):
                if first.get(eng):
                    first[eng] = False
                    return list(w) + [("T", snapA["T"])]
                return list(w)

            fxn_q = {}

            def f_loads(hf, g):
                par = g % 2
                src_sp = y_sp if hf == 0 else acc_sp
                w = []
                w2 = []
                xq = "D"
                pg, ph = (g - 2, hf) if g >= 2 else (NG - 2 + g, hf - 1)
                if pg >= 0 and ph >= 0:
                    w += [("T", m[f"gu{ph}_{pg}_11"])]
                    w2 += [("D", m[f"fsp{ph}_{pg}"])]
                    if hf == 1 and g < 2:
                        xq = "S"
                elif hf == 0 and g < 2:
                    xq = "S"
                    w += [("T", m[f"tm{NG-1}_7"]), ("V", m[f"xn1{NG-1}_7"]),
                          ("V", m[f"xn{NG-1}_7"]),
                          ("G", m[f"xnsp{NG-1}_7"])]
                    w2 += [(o, snapA[o]) for o in ENGS]
                def d_xnb(h, mark, g=g, par=par):
                    gs = slice(g * T, (g + 1) * T)
                    mark(h.dma_start(out=xnb_t[:, par], in_=xn_sp[:, :, gs]), 16)
                fxn_q[(hf, g)] = xq
                m[f"fxn{hf}_{g}"] = em.step(xq, waits=w, body=d_xnb)
                def d_yx(h, mark, g=g, par=par, src_sp=src_sp, hf=hf):
                    gs = slice(g * T, (g + 1) * T)
                    mark(h.dma_start(out=yF_t[:, par], in_=src_sp[:, :, gs]), 16)
                    if hf == 0:
                        mark(h.dma_start(out=xF_t[:, par], in_=x_h[:, :, gs]),
                             16)
                m[f"fxl{hf}_{g}"] = em.step("D", waits=w2, body=d_yx)

            hlast = None
            for hf in range(2):
                if hf > 0:
                    g_wf(hf)
                f_loads(hf, 0)
                f_loads(hf, 1)
                for g in range(NG):
                    par = g % 2
                    for hc in range(12):
                        gb, ub = (hc % 2) * 2, (hc % 2) * 2 + 1
                        w = [(fxn_q[(hf, g)], m[f"fxn{hf}_{g}"]),
                             ("G", m[f"gwf{hf}a"])]
                        if hc >= 2:
                            w.append(("V", m[f"hh{hf}_{g}_{hc-2}"]))
                        elif g > 0 or hf > 0:
                            pg, ph = (g - 1, hf) if g > 0 else (NG - 1, hf - 1)
                            w.append(("V", m[f"fev{ph}_{pg}_{ub}"]))
                        elif hf == 0 and g == 0:
                            w += [("V", m[f"y{NG-1}_7"]),
                                  ("V", m[f"r1cL{NG-1}"]),
                                  ("S", m[f"cp2b{NG-1}"])]
                        def t_gu(h, mark, hc=hc, gb=gb, ub=ub, hf=hf, g=g,
                                 par=par):
                            for dc in range(8):
                                h.matmul(bank(gb),
                                         wgb[:, dc, hc * 128:(hc + 1) * 128],
                                         xnb_t[:, par, dc, :],
                                         start=(dc == 0), stop=(dc == 7))
                            ins = None
                            for dc in range(8):
                                ins = h.matmul(bank(ub),
                                               wub[:, dc,
                                                   hc * 128:(hc + 1) * 128],
                                               xnb_t[:, par, dc, :],
                                               start=(dc == 0), stop=(dc == 7))
                            m[f"gu{hf}_{g}_{hc}"] = mark(ins)
                        em.step("T", waits=w, body=t_gu)

                        w = [("T", m[f"gu{hf}_{g}_{hc}"])]
                        if hc >= 2:
                            w.append(("V", m[f"hh{hf}_{g}_{hc-2}"]))
                        m[f"sg{hf}_{g}_{hc}"] = em.step(
                            "S", waits=bw("S", w),
                            body=lambda h, mark, hc=hc, gb=gb: mark(
                                h.activation(out=sg_t[:, hc % 2, :],
                                             in_=bank(gb), func=AF.Silu,
                                             scale=QS)))
                        m[f"hh{hf}_{g}_{hc}"] = em.step(
                            "V", waits=bw("V", [("S", m[f"sg{hf}_{g}_{hc}"])]),
                            body=lambda h, mark, hc=hc, ub=ub: mark(
                                h.scalar_tensor_tensor(
                                    out=h_bf[:, hc, :], in0=bank(ub),
                                    scalar=QS, in1=sg_t[:, hc % 2, :],
                                    op0=OP.mult, op1=OP.mult)))

                    for dc in range(8):
                        w = [("V", m[f"hh{hf}_{g}_11"]), ("G", m[f"gwf{hf}"])]
                        if g > 0 or hf > 0:
                            pg, ph = (g - 1, hf) if g > 0 else (NG - 1, hf - 1)
                            w.append(("V", m[f"fev{ph}_{pg}_{dc}"]))
                        def t_dn(h, mark, dc=dc, hf=hf, g=g):
                            ins = None
                            for hc in range(12):
                                ins = h.matmul(bank(dc),
                                               wdb[:, hc,
                                                   dc * 128:(dc + 1) * 128],
                                               h_bf[:, hc, :],
                                               start=(hc == 0), stop=(hc == 11))
                            m[f"dn{hf}_{g}_{dc}"] = mark(ins)
                        em.step("T", waits=w, body=t_dn)

                        def v_fev(h, mark, dc=dc, par=par, hf=hf, g=g):
                            stt = h.scalar_tensor_tensor(
                                out=yF_t[:, par, dc, :], in0=bank(dc),
                                scalar=QS, in1=yF_t[:, par, dc, :],
                                op0=OP.mult, op1=OP.add)
                            if hf == 0:
                                m[f"fev{hf}_{g}_{dc}"] = mark(
                                    h.tensor_tensor(out=yF_t[:, par, dc, :],
                                                    in0=yF_t[:, par, dc, :],
                                                    in1=xF_t[:, par, dc, :],
                                                    op=OP.add))
                            else:
                                m[f"fev{hf}_{g}_{dc}"] = mark(stt)
                        em.step("V", waits=[("T", m[f"dn{hf}_{g}_{dc}"]),
                                            ("D", m[f"fxl{hf}_{g}"])],
                                body=v_fev)

                    dst_sp = acc_sp
                    m[f"fsp{hf}_{g}"] = em.step(
                        "D", waits=[("V", m[f"fev{hf}_{g}_7"])],
                        body=lambda h, mark, g=g, par=par: mark(
                            h.dma_start(out=acc_sp[:, :, g * T:(g + 1) * T],
                                        in_=yF_t[:, par]), 16))
                    if g + 2 < NG:
                        f_loads(hf, g + 2)
                    elif hf == 0 and g + 2 < NG + 2:
                        pass

        snapF = em.snap()

        if "C" in phases:
            ar.reset(ffn_base)
            wkb = ar.alloc([128, 8, 2048], BF16)
            wvb = ar.alloc([128, 16, 1024], BF16)
            wcb = ar.alloc([128, 8, 1024], BF16)
            xn2_t = ar.alloc([128, 2, 8, T], BF16)
            kk_bf = ar.alloc([128, 16, T], BF16)
            cv_t = ar.alloc([128, 2, 8, T], F32)
            accC_t = ar.alloc([128, 2, 8, T], F32)
            rl_t = ar.alloc([128, 2, T], F32)
            rr2_t = ar.alloc([128, 2, T], F32)
            tc_t = ar.alloc([128, T], F32)

            firstC = {"T": True, "V": True, "S": True, "D": True}

            def bwC(eng, w):
                if firstC[eng]:
                    firstC[eng] = False
                    return list(w) + [(o, snapF[o]) for o in ENGS if o != eng]
                return list(w)

            def g_wc(cf):
                fcs = slice(cf * 16, (cf + 1) * 16)
                for j in range(4):
                    cs = slice(j * 512, (j + 1) * 512)
                    fs = slice(cf * 2048 + j * 512, cf * 2048 + (j + 1) * 512)
                    w = [("T", m[f"gu1_{NG-1}_11"])] if cf == 0 else \
                        [("T", m[f"kk0_{NG-1}_{4*j+3}"])]
                    m[f"gwck{cf}_{j}"] = em.step(
                        "G", waits=w, body=lambda h, mark, cs=cs, fs=fs: mark(
                            h.dma_start(out=wkb[:, :, cs],
                                        in_=wkey_h[:, :, fs]), 16))
                for j in range(4):
                    cs = slice(j * 256, (j + 1) * 256)
                    w = [("T", m[f"dn1_{NG-1}_7"])] if cf == 0 else \
                        [("T", m[f"cv0_{NG-1}_{2*j+1}"])]
                    def g_wv(h, mark, cs=cs, fcs=fcs, cf=cf, j=j):
                        mark(h.dma_start(out=wvb[:, :, cs],
                                         in_=wval_h[:, fcs, cs]), 16)
                        if cf == 0 and j == 3:
                            mark(h.dma_start(out=wcb[:], in_=wrec_h), 16)
                    m[f"gwcv{cf}_{j}"] = em.step("G", waits=w, body=g_wv)
                m[f"gwc{cf}"] = m[f"gwcv{cf}_3"]

            cxl_q = {}

            def c_loads(cf, g):
                par = g % 2
                w = []
                xq = "D"
                pg, pc = (g - 2, cf) if g >= 2 else (NG - 2 + g, cf - 1)
                if pg >= 0 and pc >= 0:
                    w.append(("D", m[f"csp{pc}_{pg}"] if pc == 0
                              else m[f"osp{pg}"]))
                    w.append(("T", m[f"kk{pc}_{pg}_15"] if pc == 0
                              else m[f"rr{pc}_{pg}_7"]))
                    if g < 2:
                        xq = "S"
                elif cf == 0 and g < 2:
                    xq = "S"
                    w.append(("D", m[f"fsp1_{NG-2+g}"]))
                def d_cl(h, mark, cf=cf, g=g, par=par):
                    gs = slice(g * T, (g + 1) * T)
                    if cf == 1:
                        mark(h.dma_start(out=cv_t[:, par],
                                         in_=cv_sp[:, :, gs]), 16)
                        mark(h.dma_start(out=accC_t[:, par],
                                         in_=acc_sp[:, :, gs]), 16)
                    mark(h.dma_start(out=xn2_t[:, par], in_=xn2_sp[:, :, gs]),
                         16)
                cxl_q[(cf, g)] = xq
                m[f"cxl{cf}_{g}"] = em.step(
                    xq, waits=bwC("D", w) if xq == "D" else w, body=d_cl)

            for cf in range(2):
                g_wc(cf)
                if cf == 0:
                    c_loads(cf, 0)
                    c_loads(cf, 1)
                else:
                    for fg in (0, 1):
                        def d_cvacc(h, mark, g=fg, par=fg % 2):
                            gs = slice(g * T, (g + 1) * T)
                            mark(h.dma_start(out=cv_t[:, par],
                                             in_=cv_sp[:, :, gs]), 16)
                            mark(h.dma_start(out=accC_t[:, par],
                                             in_=acc_sp[:, :, gs]), 16)
                        m[f"cxlb1_{fg}"] = em.step(
                            "D", waits=[("D", m[f"csp0_{NG-2+fg}"])],
                            body=d_cvacc)
                for g in range(NG):
                    par = g % 2
                    for fc in range(16):
                        w = [(cxl_q[(cf, g)], m[f"cxl{cf}_{g}"]),
                             ("G", m[f"gwck{cf}_{fc // 4}"])]
                        if fc >= 4:
                            w.append(("S", m[f"rl{cf}_{g}_{fc-4}"]))
                        elif g > 0 or cf > 0:
                            pg, pc = (g - 1, cf) if g > 0 else (NG - 1, cf - 1)
                            if pc == 0:
                                w.append(("V", m[f"cvev{pc}_{pg}_{fc}"]))
                            else:
                                w.append(("S", m[f"rr2{pc}_{pg}_{fc}"]))
                        else:
                            w.append(("V", m[f"fev1_{NG-1}_{fc}"]))
                        def t_kk(h, mark, fc=fc, cf=cf, g=g, par=par):
                            ins = None
                            for dc in range(8):
                                ins = h.matmul(bank(fc % 4),
                                               wkb[:, dc,
                                                   fc * 128:(fc + 1) * 128],
                                               xn2_t[:, par, dc, :],
                                               start=(dc == 0), stop=(dc == 7))
                            m[f"kk{cf}_{g}_{fc}"] = mark(ins)
                        em.step("T", waits=bwC("T", w), body=t_kk)
                        w = [("T", m[f"kk{cf}_{g}_{fc}"])]
                        if fc >= 2:
                            w.append(("V", m[f"kb{cf}_{g}_{fc-2}"]))
                        m[f"rl{cf}_{g}_{fc}"] = em.step(
                            "S", waits=bwC("S", w),
                            body=lambda h, mark, fc=fc: mark(
                                h.activation(out=rl_t[:, fc % 2, :],
                                             in_=bank(fc % 4), func=AF.Relu)))
                        m[f"kb{cf}_{g}_{fc}"] = em.step(
                            "V", waits=bwC("V", [("S", m[f"rl{cf}_{g}_{fc}"])]),
                            body=lambda h, mark, fc=fc: mark(
                                h.tensor_tensor(out=kk_bf[:, fc, :],
                                                in0=rl_t[:, fc % 2, :],
                                                in1=rl_t[:, fc % 2, :],
                                                op=OP.mult)))

                    for dc in range(8):
                        w = [("V", m[f"kb{cf}_{g}_15"]),
                             ("G", m[f"gwcv{cf}_{dc // 2}"])]
                        if cf == 0 and g > 0:
                            w.append(("V", m[f"cvev{cf}_{g-1}_{dc}"]))
                        if cf == 1:
                            w.append(("V", m[f"oev{cf}_{g-1}_{dc}"])
                                     if g > 0 else
                                     ("V", m[f"cvev0_{NG-1}_{dc}"]))
                        def t_cv(h, mark, dc=dc, cf=cf, g=g):
                            ins = None
                            for fc in range(16):
                                ins = h.matmul(bank(dc),
                                               wvb[:, fc,
                                                   dc * 128:(dc + 1) * 128],
                                               kk_bf[:, fc, :],
                                               start=(fc == 0), stop=(fc == 15))
                            m[f"cv{cf}_{g}_{dc}"] = mark(ins)
                        em.step("T", waits=w, body=t_cv)

                        if cf == 0:
                            m[f"cvev{cf}_{g}_{dc}"] = em.step(
                                "V", waits=[("T", m[f"cv{cf}_{g}_{dc}"])],
                                body=lambda h, mark, dc=dc, par=par: mark(
                                    h.tensor_copy(out=cv_t[:, par, dc, :],
                                                  in_=bank(dc))))
                        else:
                            m[f"cvt{cf}_{g}_{dc}"] = em.step(
                                "V", waits=[("T", m[f"cv{cf}_{g}_{dc}"]),
                                            ("D", m[f"cxlb1_{g}"]) if g < 2
                                            else (cxl_q[(cf, g)],
                                                  m[f"cxl{cf}_{g}"])],
                                body=lambda h, mark, dc=dc, par=par: mark(
                                    h.tensor_tensor(out=cv_t[:, par, dc, :],
                                                    in0=bank(dc),
                                                    in1=cv_t[:, par, dc, :],
                                                    op=OP.add)))
                    if cf == 0:
                        if g >= NG - 2:
                            fg = g - (NG - 2)
                            cxl_q[(1, fg)] = "D"
                            m[f"cxl1_{fg}"] = em.step(
                                "D", waits=[("T", m[f"kk0_{g}_15"])],
                                body=lambda h, mark, fg=fg: mark(
                                    h.dma_start(
                                        out=xn2_t[:, fg % 2],
                                        in_=xn2_sp[:, :,
                                                   fg * T:(fg + 1) * T]), 16))
                        m[f"csp{cf}_{g}"] = em.step(
                            "D", waits=[("V", m[f"cvev{cf}_{g}_7"])],
                            body=lambda h, mark, g=g, par=par: mark(
                                h.dma_start(out=cv_sp[:, :, g * T:(g + 1) * T],
                                            in_=cv_t[:, par]), 16))
                        if g + 2 < NG:
                            c_loads(cf, g + 2)
                        continue

                    for rc in range(8):
                        def t_rr(h, mark, rc=rc, cf=cf, g=g, par=par):
                            ins = None
                            for dc in range(8):
                                ins = h.matmul(bank(rc),
                                               wcb[:, dc,
                                                   rc * 128:(rc + 1) * 128],
                                               xn2_t[:, par, dc, :],
                                               start=(dc == 0), stop=(dc == 7))
                            m[f"rr{cf}_{g}_{rc}"] = mark(ins)
                        em.step("T", waits=[("V", m[f"cvt{cf}_{g}_{rc}"]),
                                            ("G", m[f"gwc{cf}"])], body=t_rr)
                        w = [("T", m[f"rr{cf}_{g}_{rc}"])]
                        if rc > 1:
                            w.append(("V", m[f"oev{cf}_{g}_{rc-2}"]))
                        elif g > 0:
                            w.append(("V", m[f"oev{cf}_{g-1}_{6+rc}"]))
                        m[f"rr2{cf}_{g}_{rc}"] = em.step(
                            "S", waits=w,
                            body=lambda h, mark, rc=rc: mark(
                                h.activation(out=rr2_t[:, rc % 2, :],
                                             in_=bank(rc),
                                             func=AF.Sigmoid)))
                        def v_out(h, mark, rc=rc, cf=cf, g=g, par=par):
                            h.tensor_tensor(out=tc_t[:],
                                            in0=cv_t[:, par, rc, :],
                                            in1=rr2_t[:, rc % 2, :],
                                            op=OP.mult)
                            m[f"oev{cf}_{g}_{rc}"] = mark(
                                h.tensor_tensor(out=accC_t[:, par, rc, :],
                                                in0=accC_t[:, par, rc, :],
                                                in1=tc_t[:], op=OP.add))
                        em.step("V", waits=[("S", m[f"rr2{cf}_{g}_{rc}"])],
                                body=v_out)
                    if g == NG - 1:
                        em.step("D", waits=[("V", m[f"oev{cf}_{g}_3"])],
                                body=lambda h, mark, g=g, par=par: mark(
                                    h.dma_start(
                                        out=out_h[:, 0:4, g * T:(g + 1) * T],
                                        in_=accC_t[:, par, 0:4, :]), 16))

                        def d_osp2(h, mark, g=g, par=par):
                            mark(h.dma_start(
                                out=out_h[:, 4:8, g * T:(g + 1) * T],
                                in_=accC_t[:, par, 4:8, :]), 16)
                        m[f"osp{g}"] = em.step(
                            "D", waits=[("V", m[f"oev{cf}_{g}_7"])],
                            body=d_osp2)
                    else:
                        m[f"osp{g}"] = em.step(
                            "D", waits=[("V", m[f"oev{cf}_{g}_7"])],
                            body=lambda h, mark, g=g, par=par: mark(
                                h.dma_start(out=out_h[:, :, g * T:(g + 1) * T],
                                            in_=accC_t[:, par]), 16))
                    if g + 2 < NG:
                        c_loads(cf, g + 2)

        em.step("D", waits=[("D", em.cnt["D"])])
        em.finalize()

    return nc


def _pm(a):
    a = np.ascontiguousarray(a)
    R, C = a.shape
    return np.ascontiguousarray(a.reshape(R // 128, 128, C).transpose(1, 0, 2))


def _vec(v):
    return np.ascontiguousarray(np.asarray(v, np.float32).reshape(8, 128).T)


_NC_CACHE = {}


def _get_nc(key=(8, ("A", "F", "C"), False)):
    if key not in _NC_CACHE:
        _NC_CACHE[key] = _build(NG=key[0], phases=key[1], debug=key[2])
    return _NC_CACHE[key]


def make_in_maps(inputs):
    x = np.asarray(inputs["L0_attn_x"], np.float32)
    shared = {
        "wr": _pm(np.asarray(inputs["L0_attn_Wr"], np.float32).T),
        "ww": _pm(np.asarray(inputs["L0_attn_Ww"], np.float32).T),
        "wk": _pm(np.asarray(inputs["L0_attn_Wk"], np.float32).T),
        "wv": _pm(np.asarray(inputs["L0_attn_Wv"], np.float32).T),
        "wo": _pm(np.asarray(inputs["L0_attn_Wo"], np.float32).T),
        "wrec": _pm(np.asarray(inputs["L0_attn_Wrec"], np.float32).T),
        "wkey": _pm(np.asarray(inputs["L0_attn_Wkey"], np.float32).T),
        "wval": _pm(np.asarray(inputs["L0_attn_Wval"], np.float32).T),
        "wg": _pm(np.asarray(inputs["L0_ffn_w_gate"], np.int32).astype(np.int8).T),
        "wu": _pm(np.asarray(inputs["L0_ffn_w_up"], np.int32).astype(np.int8).T),
        "wd": _pm(np.asarray(inputs["L0_ffn_w_down"], np.int32).astype(np.int8).T),
        "g1": _vec(inputs["L0_attn_ln1_g"]),
        "b1": _vec(inputs["L0_attn_ln1_b"]),
        "g2": _vec(inputs["L0_attn_ln2_g"]),
        "b2": _vec(inputs["L0_attn_ln2_b"]),
        "td": _vec(inputs["L0_attn_time_decay"]),
    }
    in_maps = []
    for b in range(8):
        mp = dict(shared)
        mp["x"] = _pm(np.ascontiguousarray(x[b].T))
        in_maps.append(mp)
    return in_maps


def kernel(**inputs):
    in_maps = make_in_maps(inputs)
    nc = _get_nc()
    res = run_bass_kernel_spmd(nc, in_maps, core_ids=list(range(8)))
    outs = []
    for b in range(8):
        o = np.asarray(res.results[b]["out"])
        outs.append(o.transpose(1, 0, 2).reshape(1024, 4096).T)
    return np.ascontiguousarray(np.stack(outs, axis=0)).astype(np.float32)


# version 24
# speedup vs baseline: 1.0216x; 1.0216x over previous
import contextlib

import numpy as np

import concourse.bass as bass
import concourse.mybir as mybir
from concourse.bass_utils import run_bass_kernel_spmd

F32 = mybir.dt.float32
BF16 = mybir.dt.bfloat16
I32 = mybir.dt.int32
I8 = mybir.dt.int8
AF = mybir.ActivationFunctionType
OP = mybir.AluOpType

LN_EPS = 1e-5
WKV_EPS = 1e-8
QS = 1.0 / 127.0
T = 512
ENGS = ("T", "V", "S", "G", "D")
_ENG_ATTR = {"T": "tensor", "V": "vector", "S": "scalar", "G": "gpsimd", "D": "sync"}
_COMPUTE = ("T", "V", "S")
_DTSZ = {F32: 4, BF16: 2, I32: 4, I8: 1}


class _MockIns:
    def then_inc(self, *a, **k):
        return self


class _MockEng:
    def __getattr__(self, name):
        return lambda *a, **k: _MockIns()


class _Em:

    PARANOID = False

    def __init__(self, nc, es, block):
        self.nc = nc
        self.block = block
        self.sem = {e: es.enter_context(nc.semaphore("sem" + e)) for e in ENGS}
        self.cnt = {e: 0 for e in ENGS}
        self.cnt2 = {e: 0 for e in ENGS}
        self._waited = {}
        self._thunks = {e: [] for e in ENGS}

    def step(self, eng, waits=(), body=None):
        if _Em.PARANOID:
            waits = [(o, self.cnt[o]) for o in ENGS if o != eng]
        fwaits = []
        for we, wv in waits:
            if wv is None or wv <= 0:
                continue
            if we == eng and eng in _COMPUTE:
                continue
            key = (eng, we)
            if self._waited.get(key, 0) >= wv:
                continue
            fwaits.append((we, wv))
            self._waited[key] = wv
        if body is not None:
            def mark1(ins, n=1):
                self.cnt[eng] += n
                return self.cnt[eng]
            body(_MockEng(), mark1)
        self._thunks[eng].append((fwaits, body))
        return self.cnt[eng]

    def finalize(self):
        for eng in ENGS:
            thunks = self._thunks[eng]

            def f(h, eng=eng, thunks=thunks):
                def mark2(ins, n=1):
                    ins.then_inc(self.sem[eng], n)
                    self.cnt2[eng] += n
                    return self.cnt2[eng]
                for fwaits, body in thunks:
                    for we, wv in fwaits:
                        h.wait_ge(self.sem[we], wv)
                    if body is not None:
                        body(h, mark2)
            getattr(self.block, _ENG_ATTR[eng])(f)
        for e in ENGS:
            assert self.cnt2[e] == self.cnt[e], (e, self.cnt2[e], self.cnt[e])

    def snap(self):
        return dict(self.cnt)


class _Arena:

    def __init__(self, ap, nbytes):
        self.ap = ap
        self.nbytes = nbytes
        self.off = 0

    def alloc(self, shape, dt, parts=128):
        n = 1
        for s in shape[1:]:
            n *= s
        sz = n * _DTSZ[dt]
        off = (self.off + 63) & ~63
        assert off + sz <= self.nbytes, (off, sz, self.nbytes)
        self.off = off + sz
        v = self.ap[0:parts, off:off + sz].bitcast(dt)
        if len(shape) == 2:
            return v
        if len(shape) == 3:
            return v.rearrange("p (c t) -> p c t", c=shape[1])
        assert len(shape) == 4
        return v.rearrange("p (a b t) -> p a b t", a=shape[1], b=shape[2])

    def mark_(self):
        return self.off

    def reset(self, off=0):
        self.off = off


def _build(NG=8, phases=("A", "F", "C"), debug=False):
    nc = bass.Bass("TRN2", target_bir_lowering=False)
    IN, OUT = "ExternalInput", "ExternalOutput"
    SCR = OUT if debug else "Internal"

    def dram(name, shape, dt, kind):
        return nc.dram_tensor(name, shape, dt, kind=kind).ap()

    x_h = dram("x", [128, 8, 4096], F32, IN)
    wA_names = ("wr", "ww", "wk", "wv", "wo")
    wA_h = {n: dram(n, [128, 8, 1024], F32, IN) for n in wA_names}
    wrec_h = dram("wrec", [128, 8, 1024], F32, IN)
    wkey_h = dram("wkey", [128, 8, 4096], F32, IN)
    wval_h = dram("wval", [128, 32, 1024], F32, IN)
    wg_h = dram("wg", [128, 8, 3072], I8, IN)
    wu_h = dram("wu", [128, 8, 3072], I8, IN)
    wd_h = dram("wd", [128, 24, 1024], I8, IN)
    g1_h = dram("g1", [128, 8], F32, IN)
    b1_h = dram("b1", [128, 8], F32, IN)
    g2_h = dram("g2", [128, 8], F32, IN)
    b2_h = dram("b2", [128, 8], F32, IN)
    td_h = dram("td", [128, 8], F32, IN)
    out_h = dram("out", [128, 8, 4096], F32, OUT)
    xn_sp = dram("xn_sp", [128, 8, 4096], BF16, SCR)
    xn2_sp = dram("xn2_sp", [128, 8, 4096], BF16, SCR)
    y_sp = dram("y_sp", [128, 8, 4096], F32, SCR)
    acc_sp = dram("acc_sp", [128, 8, 4096], F32, SCR)
    cv_sp = dram("cv_sp", [128, 8, 4096], F32, SCR)

    with contextlib.ExitStack() as es0:
        ps = es0.enter_context(nc.psum_tensor("ps", [128, 4096], F32))
        block = es0.enter_context(nc.Block())
        em = _Em(nc, es0, block)

        def bank(b):
            return ps[:, b * 512:(b + 1) * 512]

        S1a = ps[0:1, 6 * 512:6 * 512 + 512]
        S2a = ps[0:1, 7 * 512:7 * 512 + 512]
        S1b = ps[0:1, 2 * 512:2 * 512 + 512]
        S2b = ps[0:1, 3 * 512:3 * 512 + 512]

        def psb(name, shape, dt):
            return es0.enter_context(nc.sbuf_tensor(name, shape, dt))

        g1_t = psb("g1_t", [128, 8], F32)
        b1_t = psb("b1_t", [128, 8], F32)
        g2_t = psb("g2_t", [128, 8], F32)
        b2_t = psb("b2_t", [128, 8], F32)
        td_t = psb("td_t", [128, 8], F32)
        dec_t = psb("dec_t", [128, 8], F32)
        cn_t = psb("cn_t", [128, 8], F32)
        cd_t = psb("cd_t", [128, 8], F32)
        ones_bf = psb("ones_bf", [128, 1], BF16)
        ones_rf = psb("ones_rf", [1, 128], F32)

        AR_BYTES = 190 * 1024
        ar = _Arena(es0.enter_context(
            nc.sbuf_tensor("arena", [128, AR_BYTES], I8)).ap(), AR_BYTES)

        m = {}

        def d_init(h, mark):
            for ap_h, t in ((g1_h, g1_t), (b1_h, b1_t), (g2_h, g2_t),
                            (b2_h, b2_t), (td_h, td_t)):
                mark(h.dma_start(out=t[:], in_=ap_h), 16)
        m["d_vec"] = em.step("D", body=d_init)

        def v_init(h, mark):
            h.memset(ones_bf[:], 1.0)
            h.memset(ones_rf[:], 1.0)
            h.memset(cn_t[:], 0.0)
            mark(h.memset(cd_t[:], 0.0))
        m["v_init"] = em.step("V", body=v_init)

        def s_init(h, mark):
            mark(h.activation(out=dec_t[:], in_=td_t[:], func=AF.Exp))
        m["s_dec"] = em.step("S", waits=[("D", m["d_vec"])], body=s_init)

        wA = {n: ar.alloc([128, 8, 1024], BF16) for n in wA_names}
        ffn_base = 0
        x_t = ar.alloc([128, 8, T], F32)
        xb2 = ar.alloc([128, 2, T], BF16)
        sqb = ar.alloc([128, 2, T], BF16)
        rows = ar.alloc([1, 7, T], F32, parts=1)
        bc_t = ar.alloc([128, 6, T], F32)
        xn_t = ar.alloc([128, 8, T], F32)
        xn1_bf = ar.alloc([128, 8, T], BF16)
        t512 = ar.alloc([128, T], F32)
        ew_t = ar.alloc([128, 2, T], F32)
        e_t = ar.alloc([128, 2, T], F32)
        km_t = ar.alloc([128, 2, T], F32)
        num_t = ar.alloc([128, 2, T], F32)
        den_t = ar.alloc([128, T], F32)
        dd_t = ar.alloc([128, 2, T], F32)
        rn_t = ar.alloc([128, 2, T], F32)
        q_t = ar.alloc([128, T], F32)
        rw_bf = ar.alloc([128, 8, T], BF16)
        x2s_bf = ar.alloc([128, 2, T], BF16)
        amark = ar.mark_()

        def rowv(i):
            return rows[:, i, :]

        def bcv(i):
            return bc_t[:, i, :].rearrange("p (o t) -> p o t", o=1) \
                .broadcast_to([128, 8, T])

        if "A" in phases:
            def g_wA(h, mark):
                for n in wA_names:
                    mark(h.dma_start(out=wA[n][:], in_=wA_h[n]), 16)
            m["g_wA"] = em.step("G", body=g_wA)

            worder = ("wr", "ww", "wk", "wv")
            names = ("zrmm", "zwmm", "kmm", "vmm")

            def emit_sqF(g, dc):
                def v_st(h, mark, dc=dc, g=g):
                    h.tensor_copy(out=xb2[:, dc % 2, :], in_=x_t[:, dc, :])
                    mark(h.tensor_tensor(out=sqb[:, dc % 2, :],
                                         in0=xb2[:, dc % 2, :],
                                         in1=xb2[:, dc % 2, :], op=OP.mult))
                w = [("D", m[f"xld{g}_{dc}"])]
                if dc >= 2:
                    w.append(("T", m[f"st1mm{g}_{dc-2}"]))
                elif g == 1:
                    w.append(("T", m[f"st1mm0_{6+dc}"]))
                elif g > 1:
                    w.append(("T", m[f"st2mm{g-2}_{6+dc}"]))
                m[f"sqF{g}_{dc}"] = em.step("V", waits=w, body=v_st)

            def emit_st1(g, dc):
                def t_st(h, mark, dc=dc):
                    h.matmul(S1a, ones_bf[:], xb2[:, dc % 2, :],
                             start=(dc == 0), stop=(dc == 7))
                    mark(h.matmul(S2a, ones_bf[:], sqb[:, dc % 2, :],
                                  start=(dc == 0), stop=(dc == 7)))
                w = [("V", m[f"sqF{g}_{dc}"]), ("V", m["v_init"])]
                if dc == 0 and g > 0:
                    w += [("V", m[f"r1aF{g-1}"]), ("V", m[f"u{g-1}_7"])]
                m[f"st1mm{g}_{dc}"] = em.step("T", waits=w, body=t_st)

            def emit_FE_rows(g):
                def v_r1(h, mark):
                    h.tensor_scalar_mul(out=rowv(0), in0=S1a, scalar1=1.0 / 1024)
                    h.tensor_scalar(out=rowv(1), in0=S2a, scalar1=1.0 / 1024,
                                    scalar2=LN_EPS, op0=OP.mult, op1=OP.add)
                    h.tensor_tensor(out=rowv(2), in0=rowv(0), in1=rowv(0),
                                    op=OP.mult)
                    mark(h.tensor_tensor(out=rowv(3), in0=rowv(1), in1=rowv(2),
                                         op=OP.subtract))
                w = [("T", m[f"st1mm{g}_7"])]
                if g == 1:
                    w.append(("T", m["bcFmm0_3"]))
                elif g > 1:
                    w.append(("T", m[f"bc2mm{g-2}_1"]))
                m[f"r1aF{g}"] = em.step("V", waits=w, body=v_r1)

                def s_r1(h, mark):
                    h.activation(out=rowv(4), in_=rowv(3), func=AF.Ln)
                    mark(h.activation(out=rowv(5), in_=rowv(4), func=AF.Exp,
                                      scale=-0.5))
                m[f"sr1F{g}"] = em.step("S", waits=[("V", m[f"r1aF{g}"])],
                                        body=s_r1)

                def v_r2(h, mark):
                    h.tensor_tensor(out=rowv(2), in0=rowv(5), in1=rowv(5),
                                    op=OP.mult)
                    mark(h.tensor_scalar(out=rowv(3), in0=rowv(2),
                                         scalar1=-LN_EPS, scalar2=1.0 + LN_EPS,
                                         op0=OP.mult, op1=OP.add))
                m[f"r2F{g}"] = em.step("V", waits=[("S", m[f"sr1F{g}"])],
                                       body=v_r2)

                def s_r2(h, mark):
                    h.activation(out=rowv(4), in_=rowv(3), func=AF.Ln)
                    mark(h.activation(out=rowv(6), in_=rowv(4), func=AF.Exp,
                                      scale=-0.5))
                m[f"sr2F{g}"] = em.step("S", waits=[("V", m[f"r2F{g}"])],
                                        body=s_r2)

                def v_r3(h, mark):
                    h.tensor_tensor(out=rowv(4), in0=rowv(5), in1=rowv(6),
                                    op=OP.mult)
                    h.scalar_tensor_tensor(out=rowv(2), in0=rowv(0),
                                           scalar=-1.0, in1=rowv(4),
                                           op0=OP.mult, op1=OP.mult)
                    mark(h.scalar_tensor_tensor(out=rowv(1), in0=rowv(0),
                                                scalar=-1.0, in1=rowv(5),
                                                op0=OP.mult, op1=OP.mult))
                m[f"r1bF{g}"] = em.step("V", waits=[("S", m[f"sr2F{g}"])],
                                        body=v_r3)

            def emit_FE_bc(g, part):
                srcs = (4, 2, 5, 1)
                for i in (part * 2, part * 2 + 1):
                    sl = srcs[i]
                    bk = 6 + (i % 2)
                    w = [("V", m[f"r1bF{g}"])]
                    if i >= 2:
                        w.append(("S", m[f"cpF{g}_{i-2}"]))
                    m[f"bcFmm{g}_{i}"] = em.step(
                        "T", waits=w, body=lambda h, mark, bk=bk, sl=sl: mark(
                            h.matmul(bank(bk), ones_rf[:], rowv(sl),
                                     start=True, stop=True)))
                    w = [("T", m[f"bcFmm{g}_{i}"])]
                    if i < 2 and g > 0:
                        w.append(("V", m[f"xn1{g-1}_7"]))
                    if i >= 2 and g > 0:
                        w.append(("V", m[f"xn{g-1}_7"]))
                    m[f"cpF{g}_{i}"] = em.step(
                        "S", waits=w, body=lambda h, mark, bk=bk, i=i: mark(
                            h.copy(out=bc_t[:, i, :], in_=bank(bk))))

            def emit_bc2(g):
                m[f"bc2mm{g}_0"] = em.step(
                    "T", waits=[("V", m[f"r1b2L{g}"])],
                    body=lambda h, mark, g=g: mark(
                        h.matmul(bank(4), ones_rf[:], rowv(5),
                                 start=True, stop=True)))
                m[f"bc2mm{g}_1"] = em.step(
                    "T", body=lambda h, mark, g=g: mark(
                        h.matmul(bank(5), ones_rf[:], rowv(1),
                                 start=True, stop=True)))
                w = [("T", m[f"bc2mm{g}_0"])]
                if g > 0:
                    w.append(("G", m[f"x2g{g-1}_7"]))
                m[f"cp2a{g}"] = em.step(
                    "S", waits=w, body=lambda h, mark, g=g: mark(
                        h.copy(out=bc_t[:, 4, :], in_=bank(4))))
                m[f"cp2b{g}"] = em.step(
                    "S", waits=[("T", m[f"bc2mm{g}_1"])],
                    body=lambda h, mark, g=g: mark(
                        h.copy(out=bc_t[:, 5, :], in_=bank(5))))

            def emit_xn1(g, dc):
                def v_x1(h, mark, dc=dc):
                    h.tensor_tensor(out=q_t[:], in0=x_t[:, dc, :],
                                    in1=bc_t[:, 0, :], op=OP.mult)
                    h.tensor_tensor(out=q_t[:], in0=q_t[:],
                                    in1=bc_t[:, 1, :], op=OP.add)
                    mark(h.tensor_scalar(out=xn1_bf[:, dc, :], in0=q_t[:],
                                         scalar1=g1_t[:, dc:dc + 1],
                                         scalar2=b1_t[:, dc:dc + 1],
                                         op0=OP.mult, op1=OP.add))
                m[f"xn1{g}_{dc}"] = em.step(
                    "V", waits=[("S", m[f"cpF{g}_1"])], body=v_x1)

            def d_xld_full(g):
                def d_x(h, mark, g=g):
                    mk = mark(h.dma_start(
                        out=x_t[:], in_=x_h[:, :, g * T:(g + 1) * T]), 16)
                    for dc in range(8):
                        m[f"xld{g}_{dc}"] = mk
                w = []
                if g > 0:
                    w = [("V", m[f"xn{g-1}_7"]), ("V", m[f"xn1{g-1}_7"])]
                em.step("D", waits=w, body=d_x)

            if "F" in phases:
                _sv = ar.mark_()
                ar.reset(ffn_base)
                wgb = ar.alloc([128, 8, 1536], BF16)
                wub = ar.alloc([128, 8, 1536], BF16)
                wdb = ar.alloc([128, 12, 1024], BF16)
                ar.reset(_sv)

                def g_wf(hf):
                    hs = slice(hf * 1536, (hf + 1) * 1536)
                    hcs = slice(hf * 12, (hf + 1) * 12)
                    w = [("T", m[f"vmm{NG-1}_7"])] if hf == 0 else \
                        [("T", m[f"gu0_{NG-1}_11"])]
                    m[f"gwf{hf}a"] = em.step(
                        "G", waits=w, body=lambda h, mark, hs=hs: (
                            mark(h.dma_start(out=wgb[:], in_=wg_h[:, :, hs]),
                                 16),
                            mark(h.dma_start(out=wub[:], in_=wu_h[:, :, hs]),
                                 16))[-1])
                    w = [("T", m[f"tm{NG-1}_7"])] if hf == 0 else \
                        [("T", m[f"dn0_{NG-1}_7"])]
                    m[f"gwf{hf}"] = em.step(
                        "G", waits=w, body=lambda h, mark, hcs=hcs: mark(
                            h.dma_start(out=wdb[:], in_=wd_h[:, hcs, :]), 16))

            d_xld_full(0)
            for dc in range(8):
                emit_sqF(0, dc)
                emit_st1(0, dc)
            emit_FE_rows(0)
            emit_FE_bc(0, 0)
            emit_FE_bc(0, 1)
            for dc in range(8):
                emit_xn1(0, dc)

            def v_xn0(h, mark):
                h.tensor_tensor(out=xn_t[:], in0=x_t[:], in1=bcv(2),
                                op=OP.mult)
                mk = mark(h.tensor_tensor(out=xn_t[:], in0=xn_t[:],
                                          in1=bcv(3), op=OP.add))
                for dc in range(8):
                    m[f"xn0_{dc}"] = mk
            em.step("V", waits=[("S", m["cpF0_3"])], body=v_xn0)

            def g_xnsp0(h, mark):
                mk = mark(h.dma_start(out=xn_sp[:, :, 0:T], in_=xn_t[:]), 16)
                for dc in range(8):
                    m[f"xnsp0_{dc}"] = mk
            em.step("G", waits=[("V", m["xn0_7"])], body=g_xnsp0)
            d_xld_full(1)

            for g in range(NG):
                for ec in range(8):
                    base = 0 if ec % 2 == 0 else 4
                    pr = ec % 2
                    w = [("V", m[f"xn1{g}_7"])]
                    if ec == 0 and g > 0:
                        w.append(("V", m[f"r1cL{g-1}"]))
                    if ec == 1:
                        w.append(("S", m[f"cpF{g}_3"]))
                        if g > 0:
                            w.append(("S", m[f"cp2b{g-1}"]))
                    if ec >= 2:
                        w += [("V", m[f"u{g}_{ec-2}"]),
                              ("S", m[f"rn{g}_{ec-2}"])]
                    def t_proj(h, mark, ec=ec, base=base, g=g):
                        for wi, n in enumerate(worder):
                            ins = None
                            for dc in range(8):
                                ins = h.matmul(bank(base + wi),
                                               wA[n][:, dc,
                                                     ec * 128:(ec + 1) * 128],
                                               xn1_bf[:, dc, :],
                                               start=(dc == 0), stop=(dc == 7))
                            m[f"{names[wi]}{g}_{ec}"] = mark(ins)
                    em.step("T", waits=w, body=t_proj)
                    if ec == 0 and g > 0:
                        emit_bc2(g - 1)

                    w = [("T", m[f"zwmm{g}_{ec}"])]
                    if ec >= 2:
                        w.append(("V", m[f"km{g}_{ec-2}"]))
                    elif g > 0:
                        w.append(("V", m[f"km{g-1}_{ec+6}"]))
                    m[f"ew{g}_{ec}"] = em.step(
                        "S", waits=w, body=lambda h, mark, base=base, pr=pr: mark(
                            h.activation(out=ew_t[:, pr, :], in_=bank(base + 1),
                                         func=AF.Exp)))
                    if ec >= 1:
                        m[f"rcp{g}_{ec-1}"] = em.step(
                            "S", waits=[("V", m[f"vdd{g}_{ec-1}"])],
                            body=lambda h, mark, pr=1 - pr: (
                                h.activation(out=dd_t[:, pr, :],
                                             in_=dd_t[:, pr, :], func=AF.Ln),
                                mark(h.activation(out=dd_t[:, pr, :],
                                                  in_=dd_t[:, pr, :],
                                                  func=AF.Exp,
                                                  scale=-1.0)))[-1])
                    m[f"km{g}_{ec}"] = em.step(
                        "V", waits=[("T", m[f"kmm{g}_{ec}"]),
                                    ("S", m[f"ew{g}_{ec}"])],
                        body=lambda h, mark, base=base, pr=pr: mark(
                            h.tensor_tensor(out=km_t[:, pr, :],
                                            in0=bank(base + 2),
                                            in1=ew_t[:, pr, :],
                                            op=OP.subtract)))
                    w = [("V", m[f"km{g}_{ec}"])]
                    if ec >= 2:
                        w.append(("V", m[f"sce{g}_{ec-2}"]))
                    elif g > 0:
                        w.append(("V", m[f"sce{g-1}_{ec+6}"]))
                    m[f"e{g}_{ec}"] = em.step(
                        "S", waits=w, body=lambda h, mark, pr=pr: mark(
                            h.activation(out=e_t[:, pr, :], in_=km_t[:, pr, :],
                                         func=AF.Exp)))
                    w = [("T", m[f"zrmm{g}_{ec}"])]
                    if ec >= 2:
                        w.append(("V", m[f"vdd{g}_{ec-2}"]))
                    elif g > 0:
                        w.append(("V", m[f"vdd{g-1}_{ec+6}"]))
                    def s_rn(h, mark, base=base, pr=pr):
                        h.activation(out=rn_t[:, pr, :], in_=bank(base + 0),
                                     func=AF.Exp, scale=-1.0)
                        mark(h.add(out=rn_t[:, pr, :], in_=rn_t[:, pr, :],
                                   add=1.0))
                    m[f"rn{g}_{ec}"] = em.step("S", waits=w, body=s_rn)

                    def v_u(h, mark, ec=ec, base=base, g=g, pr=pr):
                        u = h.tensor_tensor(out=km_t[:, pr, :],
                                            in0=e_t[:, pr, :],
                                            in1=bank(base + 3), op=OP.mult)
                        m[f"u{g}_{ec}"] = mark(u)
                    em.step("V", waits=[("S", m[f"e{g}_{ec}"]),
                                        ("T", m[f"vmm{g}_{ec}"]),
                                        ("S", m["s_dec"])], body=v_u)

                    if ec >= 1:
                        m[f"rw{g}_{ec-1}"] = em.step(
                            "V", waits=[("S", m[f"rcp{g}_{ec-1}"])],
                            body=lambda h, mark, ec=ec, pr=1 - pr: mark(
                                h.tensor_tensor(out=rw_bf[:, ec - 1, :],
                                                in0=num_t[:, pr, :],
                                                in1=dd_t[:, pr, :],
                                                op=OP.mult)))

                    def v_scan(h, mark, ec=ec, g=g, pr=pr):
                        dcol = dec_t[:, ec:ec + 1].broadcast_to([128, T])
                        h.tensor_tensor_scan(out=num_t[:, pr, :], data0=dcol,
                                             data1=km_t[:, pr, :],
                                             initial=cn_t[:, ec:ec + 1],
                                             op0=OP.mult, op1=OP.add)
                        h.tensor_tensor_scan(out=den_t[:], data0=dcol,
                                             data1=e_t[:, pr, :],
                                             initial=cd_t[:, ec:ec + 1],
                                             op0=OP.mult, op1=OP.add)
                        h.tensor_copy(out=cn_t[:, ec:ec + 1],
                                      in_=num_t[:, pr, T - 1:T])
                        h.engine_nop()
                        mark(h.tensor_copy(out=cd_t[:, ec:ec + 1],
                                           in_=den_t[:, T - 1:T]))
                    w = []
                    if ec >= 2:
                        w.append(("S", m[f"rcp{g}_{ec-2}"]))
                    elif g > 0:
                        w.append(("S", m[f"rcp{g-1}_{ec+6}"]))
                    m[f"sce{g}_{ec}"] = em.step("V", waits=w, body=v_scan)

                    m[f"vdd{g}_{ec}"] = em.step(
                        "V", waits=[("S", m[f"rn{g}_{ec}"])],
                        body=lambda h, mark, pr=pr: mark(
                            h.scalar_tensor_tensor(out=dd_t[:, pr, :],
                                                   in0=den_t[:],
                                                   scalar=WKV_EPS,
                                                   in1=rn_t[:, pr, :],
                                                   op0=OP.add, op1=OP.mult)))

                    if g > 0:
                        def g_x2(h, mark, dc=ec, g=g):
                            h.tensor_tensor(out=t512[:],
                                            in0=xn_t[:, dc, :],
                                            in1=bc_t[:, 4, :], op=OP.mult)
                            mark(h.tensor_tensor(out=t512[:], in0=t512[:],
                                                 in1=bc_t[:, 5, :], op=OP.add))
                        w = [("S", m[f"cp2b{g-1}"]), ("S", m[f"cp2a{g-1}"])]
                        if ec >= 1:
                            w.append(("V", m[f"x2v{g-1}_{ec-1}"]))
                        elif g > 1:
                            w.append(("V", m[f"x2v{g-2}_7"]))
                        m[f"x2g{g-1}_{ec}"] = em.step("G", waits=w, body=g_x2)

                        def v_x2v(h, mark, dc=ec):
                            mark(h.tensor_scalar(out=x2s_bf[:, dc % 2, :],
                                                 in0=t512[:],
                                                 scalar1=g2_t[:, dc:dc + 1],
                                                 scalar2=b2_t[:, dc:dc + 1],
                                                 op0=OP.mult, op1=OP.add))
                        w = [("G", m[f"x2g{g-1}_{ec}"])]
                        if ec >= 2:
                            w.append(("D", m[f"xsp2{g-1}_{ec-2}"]))
                        elif g > 1:
                            w.append(("D", m[f"xsp2{g-2}_{6+ec}"]))
                        m[f"x2v{g-1}_{ec}"] = em.step("V", waits=w, body=v_x2v)
                        m[f"xsp2{g-1}_{ec}"] = em.step(
                            "D", waits=[("V", m[f"x2v{g-1}_{ec}"])],
                            body=lambda h, mark, g=g, dc=ec: mark(
                                h.dma_start(
                                    out=xn2_sp[:, dc, (g - 1) * T:g * T],
                                    in_=x2s_bf[:, dc % 2, :]), 16))

                        def v_xn(h, mark, dc=ec, g=g):
                            h.tensor_tensor(out=xn_t[:, dc, :],
                                            in0=x_t[:, dc, :],
                                            in1=bc_t[:, 2, :], op=OP.mult)
                            mark(h.tensor_tensor(out=xn_t[:, dc, :],
                                                 in0=xn_t[:, dc, :],
                                                 in1=bc_t[:, 3, :], op=OP.add))
                        w = [("S", m[f"cpF{g}_3"]), ("G", m[f"x2g{g-1}_{ec}"]),
                             ("D", m[f"ysp{g-1}"])]
                        m[f"xn{g}_{ec}"] = em.step("V", waits=w, body=v_xn)
                        m[f"xnsp{g}_{ec}"] = em.step(
                            "G", waits=[("V", m[f"xn{g}_{ec}"])],
                            body=lambda h, mark, g=g, dc=ec: mark(
                                h.dma_start(
                                    out=xn_sp[:, dc, g * T:(g + 1) * T],
                                    in_=xn_t[:, dc, :]), 16))
                        if g + 1 < NG:
                            m[f"xld{g+1}_{ec}"] = em.step(
                                "D", waits=[("V", m[f"xn{g}_{ec}"]),
                                            ("V", m[f"xn1{g}_7"])],
                                body=lambda h, mark, g=g, dc=ec: mark(
                                    h.dma_start(
                                        out=x_t[:, dc, :],
                                        in_=x_h[:, dc,
                                                (g + 1) * T:(g + 2) * T]), 16))

                def s_rcp7(h, mark):
                    h.activation(out=dd_t[:, 1, :], in_=dd_t[:, 1, :],
                                 func=AF.Ln)
                    mark(h.activation(out=dd_t[:, 1, :], in_=dd_t[:, 1, :],
                                      func=AF.Exp, scale=-1.0))
                m[f"rcp{g}_7"] = em.step(
                    "S", waits=[("V", m[f"vdd{g}_7"])], body=s_rcp7)

                if g + 1 < NG:
                    emit_sqF(g + 1, 0)
                    emit_sqF(g + 1, 1)
                m[f"rw{g}_7"] = em.step(
                    "V", waits=[("S", m[f"rcp{g}_7"])],
                    body=lambda h, mark: mark(
                        h.tensor_tensor(out=rw_bf[:, 7, :],
                                        in0=num_t[:, 1, :],
                                        in1=dd_t[:, 1, :], op=OP.mult)))
                if g + 1 < NG:
                    emit_st1(g + 1, 0)
                    emit_st1(g + 1, 1)
                    for dc in range(2, 8):
                        emit_sqF(g + 1, dc)
                        emit_st1(g + 1, dc)
                    emit_FE_rows(g + 1)

                def emit_sqL(g, dc):
                    def s_st2(h, mark, dc=dc):
                        h.activation(out=xb2[:, dc % 2, :], in_=xn_t[:, dc, :],
                                     func=AF.Copy)
                        mark(h.activation(out=sqb[:, dc % 2, :],
                                          in_=xn_t[:, dc, :], func=AF.Square))
                    w = [("V", m[f"y{g}_{dc}"])]
                    if dc >= 2:
                        w.append(("T", m[f"st2mm{g}_{dc-2}"]))
                    else:
                        w.append(("T", m[f"st1mm{g+1}_{6+dc}"] if g + 1 < NG
                                  else m[f"st1mm{g}_{6+dc}"]))
                    m[f"sqL{g}_{dc}"] = em.step("S", waits=w, body=s_st2)

                def emit_st2(g, dc):
                    def t_st2(h, mark, dc=dc):
                        h.matmul(S1b, ones_bf[:], xb2[:, dc % 2, :],
                                 start=(dc == 0), stop=(dc == 7))
                        mark(h.matmul(S2b, ones_bf[:], sqb[:, dc % 2, :],
                                      start=(dc == 0), stop=(dc == 7)))
                    w = [("S", m[f"sqL{g}_{dc}"])]
                    if dc == 0 and g > 0:
                        w.append(("V", m[f"r1cL{g-1}"]))
                    m[f"st2mm{g}_{dc}"] = em.step("T", waits=w, body=t_st2)

                for oc in range(8):
                    w = [("V", m[f"rw{g}_7"])]
                    if oc >= 2:
                        w.append(("V", m[f"y{g}_{oc-2}"]))
                    def t_wo(h, mark, oc=oc, g=g):
                        ins = None
                        for dc in range(8):
                            ins = h.matmul(bank(oc % 2),
                                           wA["wo"][:, dc,
                                                    oc * 128:(oc + 1) * 128],
                                           rw_bf[:, dc, :],
                                           start=(dc == 0), stop=(dc == 7))
                        m[f"tm{g}_{oc}"] = mark(ins)
                    em.step("T", waits=w, body=t_wo)
                    m[f"y{g}_{oc}"] = em.step(
                        "V", waits=[("T", m[f"tm{g}_{oc}"]),
                                    ("G", m[f"xnsp{g}_{oc}"])],
                        body=lambda h, mark, oc=oc: mark(
                            h.tensor_tensor(out=xn_t[:, oc, :],
                                            in0=bank(oc % 2),
                                            in1=xn_t[:, oc, :], op=OP.add)))
                    if oc == 1 and g + 1 < NG:
                        emit_FE_bc(g + 1, 0)
                    if oc >= 4:
                        emit_sqL(g, oc - 4)
                    if oc >= 2 and g + 1 < NG:
                        emit_xn1(g + 1, oc - 2)
                    if oc == 3 and g + 1 < NG:
                        emit_FE_bc(g + 1, 1)
                    if oc >= 4:
                        emit_st2(g, oc - 4)

                for dc in range(4, 8):
                    emit_sqL(g, dc)
                    if g + 1 < NG and dc < 6:
                        emit_xn1(g + 1, dc + 2)
                    emit_st2(g, dc)

                m[f"ysp{g}"] = em.step(
                    "D", waits=[("V", m[f"y{g}_7"])],
                    body=lambda h, mark, g=g: mark(
                        h.dma_start(out=y_sp[:, :, g * T:(g + 1) * T],
                                    in_=xn_t[:]), 16))
                if g == NG - 1 and "F" in phases:
                    g_wf(0)

                def v_r1b(h, mark):
                    h.tensor_scalar_mul(out=rowv(0), in0=S1b, scalar1=1.0 / 1024)
                    h.tensor_scalar(out=rowv(1), in0=S2b, scalar1=1.0 / 1024,
                                    scalar2=LN_EPS, op0=OP.mult, op1=OP.add)
                    h.tensor_tensor(out=rowv(2), in0=rowv(0), in1=rowv(0),
                                    op=OP.mult)
                    mark(h.tensor_tensor(out=rowv(3), in0=rowv(1), in1=rowv(2),
                                         op=OP.subtract))
                w = [("T", m[f"st2mm{g}_7"])]
                if g + 1 < NG:
                    w.append(("T", m[f"bcFmm{g+1}_3"]))
                m[f"r1cL{g}"] = em.step("V", waits=w, body=v_r1b)

                def s_r3(h, mark):
                    h.activation(out=rowv(4), in_=rowv(3), func=AF.Ln)
                    mark(h.activation(out=rowv(5), in_=rowv(4), func=AF.Exp,
                                      scale=-0.5))
                m[f"sr3L{g}"] = em.step("S", waits=[("V", m[f"r1cL{g}"])],
                                        body=s_r3)

                def v_r4(h, mark):
                    mark(h.scalar_tensor_tensor(out=rowv(1), in0=rowv(0),
                                                scalar=-1.0, in1=rowv(5),
                                                op0=OP.mult, op1=OP.mult))
                m[f"r1b2L{g}"] = em.step("V", waits=[("S", m[f"sr3L{g}"])],
                                         body=v_r4)

                if g == NG - 1:
                    emit_bc2(g)

            g = NG - 1
            for dc in range(8):
                def g_x2e(h, mark, dc=dc, g=g):
                    h.tensor_tensor(out=t512[:], in0=xn_t[:, dc, :],
                                    in1=bc_t[:, 4, :], op=OP.mult)
                    mark(h.tensor_tensor(out=t512[:], in0=t512[:],
                                         in1=bc_t[:, 5, :], op=OP.add))
                w = [("S", m[f"cp2b{g}"]), ("S", m[f"cp2a{g}"])]
                if dc >= 1:
                    w.append(("V", m[f"x2v{g}_{dc-1}"]))
                else:
                    w.append(("V", m[f"x2v{g-1}_7"]))
                m[f"x2g{g}_{dc}"] = em.step("V", waits=w, body=g_x2e)

                def v_x2v(h, mark, dc=dc):
                    mark(h.tensor_scalar(out=x2s_bf[:, dc % 2, :],
                                         in0=t512[:],
                                         scalar1=g2_t[:, dc:dc + 1],
                                         scalar2=b2_t[:, dc:dc + 1],
                                         op0=OP.mult, op1=OP.add))
                w = [("G", m[f"x2g{g}_{dc}"])]
                if dc >= 2:
                    w.append(("D", m[f"xsp2{g}_{dc-2}"]))
                else:
                    w.append(("D", m[f"xsp2{g-1}_{6+dc}"]))
                m[f"x2v{g}_{dc}"] = em.step("V", waits=w, body=v_x2v)
                m[f"xsp2{g}_{dc}"] = em.step(
                    "D", waits=[("V", m[f"x2v{g}_{dc}"])],
                    body=lambda h, mark, g=g, dc=dc: mark(
                        h.dma_start(out=xn2_sp[:, dc, g * T:(g + 1) * T],
                                    in_=x2s_bf[:, dc % 2, :]), 16))

        snapA = em.snap()

        if "F" in phases:
            ar.reset(ffn_base)
            ar.alloc([128, 8, 1536], BF16)
            ar.alloc([128, 8, 1536], BF16)
            ar.alloc([128, 12, 1024], BF16)
            xnb_t = ar.alloc([128, 2, 8, T], BF16)
            yF_t = ar.alloc([128, 2, 8, T], F32)
            xF_t = ar.alloc([128, 2, 8, T], F32)
            h_bf = ar.alloc([128, 12, T], BF16)
            sg_t = ar.alloc([128, 2, T], F32)

            first = {"V": True, "S": True}

            def bw(eng, w):
                if first.get(eng):
                    first[eng] = False
                    return list(w) + [("T", snapA["T"])]
                return list(w)

            fxn_q = {}

            def f_loads(hf, g):
                par = g % 2
                src_sp = y_sp if hf == 0 else acc_sp
                w = []
                w2 = []
                xq = "D"
                pg, ph = (g - 2, hf) if g >= 2 else (NG - 2 + g, hf - 1)
                if pg >= 0 and ph >= 0:
                    w += [("T", m[f"gu{ph}_{pg}_11"])]
                    w2 += [("D", m[f"fsp{ph}_{pg}"])]
                    if hf == 1 and g < 2:
                        xq = "S"
                elif hf == 0 and g < 2:
                    xq = "S"
                    w += [("T", m[f"tm{NG-1}_7"]), ("V", m[f"xn1{NG-1}_7"]),
                          ("V", m[f"xn{NG-1}_7"]),
                          ("G", m[f"xnsp{NG-1}_7"])]
                    w2 += [(o, snapA[o]) for o in ENGS]
                def d_xnb(h, mark, g=g, par=par):
                    gs = slice(g * T, (g + 1) * T)
                    mark(h.dma_start(out=xnb_t[:, par], in_=xn_sp[:, :, gs]), 16)
                fxn_q[(hf, g)] = xq
                m[f"fxn{hf}_{g}"] = em.step(xq, waits=w, body=d_xnb)
                def d_yx(h, mark, g=g, par=par, src_sp=src_sp, hf=hf):
                    gs = slice(g * T, (g + 1) * T)
                    mark(h.dma_start(out=yF_t[:, par], in_=src_sp[:, :, gs]), 16)
                    if hf == 0:
                        mark(h.dma_start(out=xF_t[:, par], in_=x_h[:, :, gs]),
                             16)
                m[f"fxl{hf}_{g}"] = em.step("D", waits=w2, body=d_yx)

            hlast = None
            for hf in range(2):
                if hf > 0:
                    g_wf(hf)
                f_loads(hf, 0)
                f_loads(hf, 1)
                for g in range(NG):
                    par = g % 2
                    for hc in range(12):
                        gb, ub = (hc % 2) * 2, (hc % 2) * 2 + 1
                        w = [(fxn_q[(hf, g)], m[f"fxn{hf}_{g}"]),
                             ("G", m[f"gwf{hf}a"])]
                        if hc >= 2:
                            w.append(("V", m[f"hh{hf}_{g}_{hc-2}"]))
                        elif g > 0 or hf > 0:
                            pg, ph = (g - 1, hf) if g > 0 else (NG - 1, hf - 1)
                            w.append(("V", m[f"fev{ph}_{pg}_{ub}"]))
                        elif hf == 0 and g == 0:
                            w += [("V", m[f"y{NG-1}_7"]),
                                  ("V", m[f"r1cL{NG-1}"]),
                                  ("S", m[f"cp2b{NG-1}"])]
                        def t_gu(h, mark, hc=hc, gb=gb, ub=ub, hf=hf, g=g,
                                 par=par):
                            for dc in range(8):
                                h.matmul(bank(gb),
                                         wgb[:, dc, hc * 128:(hc + 1) * 128],
                                         xnb_t[:, par, dc, :],
                                         start=(dc == 0), stop=(dc == 7))
                            ins = None
                            for dc in range(8):
                                ins = h.matmul(bank(ub),
                                               wub[:, dc,
                                                   hc * 128:(hc + 1) * 128],
                                               xnb_t[:, par, dc, :],
                                               start=(dc == 0), stop=(dc == 7))
                            m[f"gu{hf}_{g}_{hc}"] = mark(ins)
                        em.step("T", waits=w, body=t_gu)

                        w = [("T", m[f"gu{hf}_{g}_{hc}"])]
                        if hc >= 2:
                            w.append(("V", m[f"hh{hf}_{g}_{hc-2}"]))
                        m[f"sg{hf}_{g}_{hc}"] = em.step(
                            "S", waits=bw("S", w),
                            body=lambda h, mark, hc=hc, gb=gb: mark(
                                h.activation(out=sg_t[:, hc % 2, :],
                                             in_=bank(gb), func=AF.Silu,
                                             scale=QS)))
                        m[f"hh{hf}_{g}_{hc}"] = em.step(
                            "V", waits=bw("V", [("S", m[f"sg{hf}_{g}_{hc}"])]),
                            body=lambda h, mark, hc=hc, ub=ub: mark(
                                h.scalar_tensor_tensor(
                                    out=h_bf[:, hc, :], in0=bank(ub),
                                    scalar=QS, in1=sg_t[:, hc % 2, :],
                                    op0=OP.mult, op1=OP.mult)))

                    for dc in range(8):
                        w = [("V", m[f"hh{hf}_{g}_11"]), ("G", m[f"gwf{hf}"])]
                        if g > 0 or hf > 0:
                            pg, ph = (g - 1, hf) if g > 0 else (NG - 1, hf - 1)
                            w.append(("V", m[f"fev{ph}_{pg}_{dc}"]))
                        def t_dn(h, mark, dc=dc, hf=hf, g=g):
                            ins = None
                            for hc in range(12):
                                ins = h.matmul(bank(dc),
                                               wdb[:, hc,
                                                   dc * 128:(dc + 1) * 128],
                                               h_bf[:, hc, :],
                                               start=(hc == 0), stop=(hc == 11))
                            m[f"dn{hf}_{g}_{dc}"] = mark(ins)
                        em.step("T", waits=w, body=t_dn)

                        def v_fev(h, mark, dc=dc, par=par, hf=hf, g=g):
                            stt = h.scalar_tensor_tensor(
                                out=yF_t[:, par, dc, :], in0=bank(dc),
                                scalar=QS, in1=yF_t[:, par, dc, :],
                                op0=OP.mult, op1=OP.add)
                            if hf == 0:
                                m[f"fev{hf}_{g}_{dc}"] = mark(
                                    h.tensor_tensor(out=yF_t[:, par, dc, :],
                                                    in0=yF_t[:, par, dc, :],
                                                    in1=xF_t[:, par, dc, :],
                                                    op=OP.add))
                            else:
                                m[f"fev{hf}_{g}_{dc}"] = mark(stt)
                        em.step("V", waits=[("T", m[f"dn{hf}_{g}_{dc}"]),
                                            ("D", m[f"fxl{hf}_{g}"])],
                                body=v_fev)

                    dst_sp = acc_sp
                    m[f"fsp{hf}_{g}"] = em.step(
                        "D", waits=[("V", m[f"fev{hf}_{g}_7"])],
                        body=lambda h, mark, g=g, par=par: mark(
                            h.dma_start(out=acc_sp[:, :, g * T:(g + 1) * T],
                                        in_=yF_t[:, par]), 16))
                    if g + 2 < NG:
                        f_loads(hf, g + 2)
                    elif hf == 0 and g + 2 < NG + 2:
                        pass

        snapF = em.snap()

        if "C" in phases:
            ar.reset(ffn_base)
            wkb = ar.alloc([128, 8, 2048], BF16)
            wvb = ar.alloc([128, 16, 1024], BF16)
            wcb = ar.alloc([128, 8, 1024], BF16)
            xn2_t = ar.alloc([128, 2, 8, T], BF16)
            kk_bf = ar.alloc([128, 16, T], BF16)
            cv_t = ar.alloc([128, 2, 8, T], F32)
            accC_t = ar.alloc([128, 2, 8, T], F32)
            rl_t = ar.alloc([128, 2, T], F32)
            rr2_t = ar.alloc([128, T], F32)
            tc_t = ar.alloc([128, T], F32)

            firstC = {"T": True, "V": True, "S": True, "D": True}

            def bwC(eng, w):
                if firstC[eng]:
                    firstC[eng] = False
                    return list(w) + [(o, snapF[o]) for o in ENGS if o != eng]
                return list(w)

            def g_wc(cf):
                fcs = slice(cf * 16, (cf + 1) * 16)
                for j in range(4):
                    cs = slice(j * 512, (j + 1) * 512)
                    fs = slice(cf * 2048 + j * 512, cf * 2048 + (j + 1) * 512)
                    w = [("T", m[f"gu1_{NG-1}_11"])] if cf == 0 else \
                        [("T", m[f"kk0_{NG-1}_{4*j+3}"])]
                    m[f"gwck{cf}_{j}"] = em.step(
                        "G", waits=w, body=lambda h, mark, cs=cs, fs=fs: mark(
                            h.dma_start(out=wkb[:, :, cs],
                                        in_=wkey_h[:, :, fs]), 16))
                for j in range(4):
                    cs = slice(j * 256, (j + 1) * 256)
                    w = [("T", m[f"dn1_{NG-1}_7"])] if cf == 0 else \
                        [("T", m[f"cv0_{NG-1}_{2*j+1}"])]
                    def g_wv(h, mark, cs=cs, fcs=fcs, cf=cf, j=j):
                        mark(h.dma_start(out=wvb[:, :, cs],
                                         in_=wval_h[:, fcs, cs]), 16)
                        if cf == 0 and j == 3:
                            mark(h.dma_start(out=wcb[:], in_=wrec_h), 16)
                    m[f"gwcv{cf}_{j}"] = em.step("G", waits=w, body=g_wv)
                m[f"gwc{cf}"] = m[f"gwcv{cf}_3"]

            cxl_q = {}

            def c_loads(cf, g):
                par = g % 2
                w = []
                xq = "D"
                pg, pc = (g - 2, cf) if g >= 2 else (NG - 2 + g, cf - 1)
                if pg >= 0 and pc >= 0:
                    w.append(("D", m[f"csp{pc}_{pg}"] if pc == 0
                              else m[f"osp{pg}"]))
                    w.append(("T", m[f"kk{pc}_{pg}_15"] if pc == 0
                              else m[f"rr{pc}_{pg}_7"]))
                    if g < 2:
                        xq = "S"
                elif cf == 0 and g < 2:
                    xq = "S"
                    w.append(("D", m[f"fsp1_{NG-2+g}"]))
                def d_cl(h, mark, cf=cf, g=g, par=par):
                    gs = slice(g * T, (g + 1) * T)
                    if cf == 1:
                        mark(h.dma_start(out=cv_t[:, par],
                                         in_=cv_sp[:, :, gs]), 16)
                        mark(h.dma_start(out=accC_t[:, par],
                                         in_=acc_sp[:, :, gs]), 16)
                    mark(h.dma_start(out=xn2_t[:, par], in_=xn2_sp[:, :, gs]),
                         16)
                cxl_q[(cf, g)] = xq
                m[f"cxl{cf}_{g}"] = em.step(
                    xq, waits=bwC("D", w) if xq == "D" else w, body=d_cl)

            for cf in range(2):
                g_wc(cf)
                if cf == 0:
                    c_loads(cf, 0)
                    c_loads(cf, 1)
                else:
                    for fg in (0, 1):
                        def d_cvacc(h, mark, g=fg, par=fg % 2):
                            gs = slice(g * T, (g + 1) * T)
                            mark(h.dma_start(out=cv_t[:, par],
                                             in_=cv_sp[:, :, gs]), 16)
                            mark(h.dma_start(out=accC_t[:, par],
                                             in_=acc_sp[:, :, gs]), 16)
                        m[f"cxlb1_{fg}"] = em.step(
                            "D", waits=[("D", m[f"csp0_{NG-2+fg}"])],
                            body=d_cvacc)
                for g in range(NG):
                    par = g % 2
                    for fc in range(16):
                        w = [(cxl_q[(cf, g)], m[f"cxl{cf}_{g}"]),
                             ("G", m[f"gwck{cf}_{fc // 4}"])]
                        if fc >= 4:
                            w.append(("S", m[f"rl{cf}_{g}_{fc-4}"]))
                        elif g > 0 or cf > 0:
                            pg, pc = (g - 1, cf) if g > 0 else (NG - 1, cf - 1)
                            if pc == 0:
                                w.append(("V", m[f"cvev{pc}_{pg}_{fc}"]))
                            else:
                                w.append(("S", m[f"rr2{pc}_{pg}_{fc}"]))
                        else:
                            w.append(("V", m[f"fev1_{NG-1}_{fc}"]))
                        def t_kk(h, mark, fc=fc, cf=cf, g=g, par=par):
                            ins = None
                            for dc in range(8):
                                ins = h.matmul(bank(fc % 4),
                                               wkb[:, dc,
                                                   fc * 128:(fc + 1) * 128],
                                               xn2_t[:, par, dc, :],
                                               start=(dc == 0), stop=(dc == 7))
                            m[f"kk{cf}_{g}_{fc}"] = mark(ins)
                        em.step("T", waits=bwC("T", w), body=t_kk)
                        w = [("T", m[f"kk{cf}_{g}_{fc}"])]
                        if fc >= 2:
                            w.append(("V", m[f"kb{cf}_{g}_{fc-2}"]))
                        m[f"rl{cf}_{g}_{fc}"] = em.step(
                            "S", waits=bwC("S", w),
                            body=lambda h, mark, fc=fc: mark(
                                h.activation(out=rl_t[:, fc % 2, :],
                                             in_=bank(fc % 4), func=AF.Relu)))
                        m[f"kb{cf}_{g}_{fc}"] = em.step(
                            "V", waits=bwC("V", [("S", m[f"rl{cf}_{g}_{fc}"])]),
                            body=lambda h, mark, fc=fc: mark(
                                h.tensor_tensor(out=kk_bf[:, fc, :],
                                                in0=rl_t[:, fc % 2, :],
                                                in1=rl_t[:, fc % 2, :],
                                                op=OP.mult)))

                    for dc in range(8):
                        w = [("V", m[f"kb{cf}_{g}_15"]),
                             ("G", m[f"gwcv{cf}_{dc // 2}"])]
                        if cf == 0 and g > 0:
                            w.append(("V", m[f"cvev{cf}_{g-1}_{dc}"]))
                        if cf == 1:
                            w.append(("V", m[f"oev{cf}_{g-1}_{dc}"])
                                     if g > 0 else
                                     ("V", m[f"cvev0_{NG-1}_{dc}"]))
                        def t_cv(h, mark, dc=dc, cf=cf, g=g):
                            ins = None
                            for fc in range(16):
                                ins = h.matmul(bank(dc),
                                               wvb[:, fc,
                                                   dc * 128:(dc + 1) * 128],
                                               kk_bf[:, fc, :],
                                               start=(fc == 0), stop=(fc == 15))
                            m[f"cv{cf}_{g}_{dc}"] = mark(ins)
                        em.step("T", waits=w, body=t_cv)

                        if cf == 0:
                            m[f"cvev{cf}_{g}_{dc}"] = em.step(
                                "V", waits=[("T", m[f"cv{cf}_{g}_{dc}"])],
                                body=lambda h, mark, dc=dc, par=par: mark(
                                    h.tensor_copy(out=cv_t[:, par, dc, :],
                                                  in_=bank(dc))))
                        else:
                            m[f"cvt{cf}_{g}_{dc}"] = em.step(
                                "V", waits=[("T", m[f"cv{cf}_{g}_{dc}"]),
                                            ("D", m[f"cxlb1_{g}"]) if g < 2
                                            else (cxl_q[(cf, g)],
                                                  m[f"cxl{cf}_{g}"])],
                                body=lambda h, mark, dc=dc, par=par: mark(
                                    h.tensor_tensor(out=cv_t[:, par, dc, :],
                                                    in0=bank(dc),
                                                    in1=cv_t[:, par, dc, :],
                                                    op=OP.add)))
                    if cf == 0:
                        if g >= NG - 2:
                            fg = g - (NG - 2)
                            cxl_q[(1, fg)] = "D"
                            m[f"cxl1_{fg}"] = em.step(
                                "D", waits=[("T", m[f"kk0_{g}_15"])],
                                body=lambda h, mark, fg=fg: mark(
                                    h.dma_start(
                                        out=xn2_t[:, fg % 2],
                                        in_=xn2_sp[:, :,
                                                   fg * T:(fg + 1) * T]), 16))
                        m[f"csp{cf}_{g}"] = em.step(
                            "D", waits=[("V", m[f"cvev{cf}_{g}_7"])],
                            body=lambda h, mark, g=g, par=par: mark(
                                h.dma_start(out=cv_sp[:, :, g * T:(g + 1) * T],
                                            in_=cv_t[:, par]), 16))
                        if g + 2 < NG:
                            c_loads(cf, g + 2)
                        continue

                    for rc in range(8):
                        def t_rr(h, mark, rc=rc, cf=cf, g=g, par=par):
                            ins = None
                            for dc in range(8):
                                ins = h.matmul(bank(rc),
                                               wcb[:, dc,
                                                   rc * 128:(rc + 1) * 128],
                                               xn2_t[:, par, dc, :],
                                               start=(dc == 0), stop=(dc == 7))
                            m[f"rr{cf}_{g}_{rc}"] = mark(ins)
                        em.step("T", waits=[("V", m[f"cvt{cf}_{g}_{rc}"]),
                                            ("G", m[f"gwc{cf}"])], body=t_rr)
                        w = [("T", m[f"rr{cf}_{g}_{rc}"])]
                        if rc > 0:
                            w.append(("V", m[f"oev{cf}_{g}_{rc-1}"]))
                        elif g > 0:
                            w.append(("V", m[f"oev{cf}_{g-1}_7"]))
                        m[f"rr2{cf}_{g}_{rc}"] = em.step(
                            "S", waits=w,
                            body=lambda h, mark, rc=rc: mark(
                                h.activation(out=rr2_t[:], in_=bank(rc),
                                             func=AF.Sigmoid)))
                        def v_out(h, mark, rc=rc, cf=cf, g=g, par=par):
                            h.tensor_tensor(out=tc_t[:],
                                            in0=cv_t[:, par, rc, :],
                                            in1=rr2_t[:], op=OP.mult)
                            m[f"oev{cf}_{g}_{rc}"] = mark(
                                h.tensor_tensor(out=accC_t[:, par, rc, :],
                                                in0=accC_t[:, par, rc, :],
                                                in1=tc_t[:], op=OP.add))
                        em.step("V", waits=[("S", m[f"rr2{cf}_{g}_{rc}"])],
                                body=v_out)
                    if g == NG - 1:
                        em.step("D", waits=[("V", m[f"oev{cf}_{g}_3"])],
                                body=lambda h, mark, g=g, par=par: mark(
                                    h.dma_start(
                                        out=out_h[:, 0:4, g * T:(g + 1) * T],
                                        in_=accC_t[:, par, 0:4, :]), 16))

                        def d_osp2(h, mark, g=g, par=par):
                            mark(h.dma_start(
                                out=out_h[:, 4:8, g * T:(g + 1) * T],
                                in_=accC_t[:, par, 4:8, :]), 16)
                        m[f"osp{g}"] = em.step(
                            "D", waits=[("V", m[f"oev{cf}_{g}_7"])],
                            body=d_osp2)
                    else:
                        m[f"osp{g}"] = em.step(
                            "D", waits=[("V", m[f"oev{cf}_{g}_7"])],
                            body=lambda h, mark, g=g, par=par: mark(
                                h.dma_start(out=out_h[:, :, g * T:(g + 1) * T],
                                            in_=accC_t[:, par]), 16))
                    if g + 2 < NG:
                        c_loads(cf, g + 2)

        em.step("D", waits=[("D", em.cnt["D"])])
        em.finalize()

    return nc


def _pm(a):
    a = np.ascontiguousarray(a)
    R, C = a.shape
    return np.ascontiguousarray(a.reshape(R // 128, 128, C).transpose(1, 0, 2))


def _vec(v):
    return np.ascontiguousarray(np.asarray(v, np.float32).reshape(8, 128).T)


_NC_CACHE = {}


def _get_nc(key=(8, ("A", "F", "C"), False)):
    if key not in _NC_CACHE:
        _NC_CACHE[key] = _build(NG=key[0], phases=key[1], debug=key[2])
    return _NC_CACHE[key]


def make_in_maps(inputs):
    x = np.asarray(inputs["L0_attn_x"], np.float32)
    shared = {
        "wr": _pm(np.asarray(inputs["L0_attn_Wr"], np.float32).T),
        "ww": _pm(np.asarray(inputs["L0_attn_Ww"], np.float32).T),
        "wk": _pm(np.asarray(inputs["L0_attn_Wk"], np.float32).T),
        "wv": _pm(np.asarray(inputs["L0_attn_Wv"], np.float32).T),
        "wo": _pm(np.asarray(inputs["L0_attn_Wo"], np.float32).T),
        "wrec": _pm(np.asarray(inputs["L0_attn_Wrec"], np.float32).T),
        "wkey": _pm(np.asarray(inputs["L0_attn_Wkey"], np.float32).T),
        "wval": _pm(np.asarray(inputs["L0_attn_Wval"], np.float32).T),
        "wg": _pm(np.asarray(inputs["L0_ffn_w_gate"], np.int32).astype(np.int8).T),
        "wu": _pm(np.asarray(inputs["L0_ffn_w_up"], np.int32).astype(np.int8).T),
        "wd": _pm(np.asarray(inputs["L0_ffn_w_down"], np.int32).astype(np.int8).T),
        "g1": _vec(inputs["L0_attn_ln1_g"]),
        "b1": _vec(inputs["L0_attn_ln1_b"]),
        "g2": _vec(inputs["L0_attn_ln2_g"]),
        "b2": _vec(inputs["L0_attn_ln2_b"]),
        "td": _vec(inputs["L0_attn_time_decay"]),
    }
    in_maps = []
    for b in range(8):
        mp = dict(shared)
        mp["x"] = _pm(np.ascontiguousarray(x[b].T))
        in_maps.append(mp)
    return in_maps


def kernel(**inputs):
    in_maps = make_in_maps(inputs)
    nc = _get_nc()
    res = run_bass_kernel_spmd(nc, in_maps, core_ids=list(range(8)))
    outs = []
    for b in range(8):
        o = np.asarray(res.results[b]["out"])
        outs.append(o.transpose(1, 0, 2).reshape(1024, 4096).T)
    return np.ascontiguousarray(np.stack(outs, axis=0)).astype(np.float32)


# version 25
# speedup vs baseline: 1.0259x; 1.0043x over previous
import contextlib

import numpy as np

import concourse.bass as bass
import concourse.mybir as mybir
from concourse.bass_utils import run_bass_kernel_spmd

F32 = mybir.dt.float32
BF16 = mybir.dt.bfloat16
I32 = mybir.dt.int32
I8 = mybir.dt.int8
AF = mybir.ActivationFunctionType
OP = mybir.AluOpType

LN_EPS = 1e-5
WKV_EPS = 1e-8
QS = 1.0 / 127.0
T = 512
ENGS = ("T", "V", "S", "G", "D")
_ENG_ATTR = {"T": "tensor", "V": "vector", "S": "scalar", "G": "gpsimd", "D": "sync"}
_COMPUTE = ("T", "V", "S")
_DTSZ = {F32: 4, BF16: 2, I32: 4, I8: 1}


class _MockIns:
    def then_inc(self, *a, **k):
        return self


class _MockEng:
    def __getattr__(self, name):
        return lambda *a, **k: _MockIns()


class _Em:

    PARANOID = False

    def __init__(self, nc, es, block):
        self.nc = nc
        self.block = block
        self.sem = {e: es.enter_context(nc.semaphore("sem" + e)) for e in ENGS}
        self.cnt = {e: 0 for e in ENGS}
        self.cnt2 = {e: 0 for e in ENGS}
        self._waited = {}
        self._thunks = {e: [] for e in ENGS}

    def step(self, eng, waits=(), body=None):
        if _Em.PARANOID:
            waits = [(o, self.cnt[o]) for o in ENGS if o != eng]
        fwaits = []
        for we, wv in waits:
            if wv is None or wv <= 0:
                continue
            if we == eng and eng in _COMPUTE:
                continue
            key = (eng, we)
            if self._waited.get(key, 0) >= wv:
                continue
            fwaits.append((we, wv))
            self._waited[key] = wv
        if body is not None:
            def mark1(ins, n=1):
                self.cnt[eng] += n
                return self.cnt[eng]
            body(_MockEng(), mark1)
        self._thunks[eng].append((fwaits, body))
        return self.cnt[eng]

    def finalize(self):
        for eng in ENGS:
            thunks = self._thunks[eng]

            def f(h, eng=eng, thunks=thunks):
                def mark2(ins, n=1):
                    ins.then_inc(self.sem[eng], n)
                    self.cnt2[eng] += n
                    return self.cnt2[eng]
                for fwaits, body in thunks:
                    for we, wv in fwaits:
                        h.wait_ge(self.sem[we], wv)
                    if body is not None:
                        body(h, mark2)
            getattr(self.block, _ENG_ATTR[eng])(f)
        for e in ENGS:
            assert self.cnt2[e] == self.cnt[e], (e, self.cnt2[e], self.cnt[e])

    def snap(self):
        return dict(self.cnt)


class _Arena:

    def __init__(self, ap, nbytes):
        self.ap = ap
        self.nbytes = nbytes
        self.off = 0

    def alloc(self, shape, dt, parts=128):
        n = 1
        for s in shape[1:]:
            n *= s
        sz = n * _DTSZ[dt]
        off = (self.off + 63) & ~63
        assert off + sz <= self.nbytes, (off, sz, self.nbytes)
        self.off = off + sz
        v = self.ap[0:parts, off:off + sz].bitcast(dt)
        if len(shape) == 2:
            return v
        if len(shape) == 3:
            return v.rearrange("p (c t) -> p c t", c=shape[1])
        assert len(shape) == 4
        return v.rearrange("p (a b t) -> p a b t", a=shape[1], b=shape[2])

    def mark_(self):
        return self.off

    def reset(self, off=0):
        self.off = off


def _build(NG=8, phases=("A", "F", "C"), debug=False):
    nc = bass.Bass("TRN2", target_bir_lowering=False)
    IN, OUT = "ExternalInput", "ExternalOutput"
    SCR = OUT if debug else "Internal"

    def dram(name, shape, dt, kind):
        return nc.dram_tensor(name, shape, dt, kind=kind).ap()

    x_h = dram("x", [128, 8, 4096], F32, IN)
    wA_names = ("wr", "ww", "wk", "wv", "wo")
    wA_h = {n: dram(n, [128, 8, 1024], F32, IN) for n in wA_names}
    wrec_h = dram("wrec", [128, 8, 1024], F32, IN)
    wkey_h = dram("wkey", [128, 8, 4096], F32, IN)
    wval_h = dram("wval", [128, 32, 1024], F32, IN)
    wg_h = dram("wg", [128, 8, 3072], I8, IN)
    wu_h = dram("wu", [128, 8, 3072], I8, IN)
    wd_h = dram("wd", [128, 24, 1024], I8, IN)
    g1_h = dram("g1", [128, 8], F32, IN)
    b1_h = dram("b1", [128, 8], F32, IN)
    g2_h = dram("g2", [128, 8], F32, IN)
    b2_h = dram("b2", [128, 8], F32, IN)
    td_h = dram("td", [128, 8], F32, IN)
    out_h = dram("out", [128, 8, 4096], F32, OUT)
    xn_sp = dram("xn_sp", [128, 8, 4096], BF16, SCR)
    xn2_sp = dram("xn2_sp", [128, 8, 4096], BF16, SCR)
    y_sp = dram("y_sp", [128, 8, 4096], F32, SCR)
    acc_sp = dram("acc_sp", [128, 8, 4096], F32, SCR)
    cv_sp = dram("cv_sp", [128, 8, 4096], F32, SCR)

    with contextlib.ExitStack() as es0:
        ps = es0.enter_context(nc.psum_tensor("ps", [128, 4096], F32))
        block = es0.enter_context(nc.Block())
        em = _Em(nc, es0, block)

        def bank(b):
            return ps[:, b * 512:(b + 1) * 512]

        S1a = ps[0:1, 6 * 512:6 * 512 + 512]
        S2a = ps[0:1, 7 * 512:7 * 512 + 512]
        S1b = ps[0:1, 2 * 512:2 * 512 + 512]
        S2b = ps[0:1, 3 * 512:3 * 512 + 512]

        def psb(name, shape, dt):
            return es0.enter_context(nc.sbuf_tensor(name, shape, dt))

        g1_t = psb("g1_t", [128, 8], F32)
        b1_t = psb("b1_t", [128, 8], F32)
        g2_t = psb("g2_t", [128, 8], F32)
        b2_t = psb("b2_t", [128, 8], F32)
        td_t = psb("td_t", [128, 8], F32)
        dec_t = psb("dec_t", [128, 8], F32)
        cn_t = psb("cn_t", [128, 8], F32)
        cd_t = psb("cd_t", [128, 8], F32)
        ones_bf = psb("ones_bf", [128, 1], BF16)
        ones_rf = psb("ones_rf", [1, 128], F32)

        AR_BYTES = 190 * 1024
        ar = _Arena(es0.enter_context(
            nc.sbuf_tensor("arena", [128, AR_BYTES], I8)).ap(), AR_BYTES)

        m = {}

        def d_init(h, mark):
            for ap_h, t in ((g1_h, g1_t), (b1_h, b1_t), (g2_h, g2_t),
                            (b2_h, b2_t), (td_h, td_t)):
                mark(h.dma_start(out=t[:], in_=ap_h), 16)
        m["d_vec"] = em.step("D", body=d_init)

        def v_init(h, mark):
            h.memset(ones_bf[:], 1.0)
            h.memset(ones_rf[:], 1.0)
            h.memset(cn_t[:], 0.0)
            mark(h.memset(cd_t[:], 0.0))
        m["v_init"] = em.step("V", body=v_init)

        def s_init(h, mark):
            mark(h.activation(out=dec_t[:], in_=td_t[:], func=AF.Exp))
        m["s_dec"] = em.step("S", waits=[("D", m["d_vec"])], body=s_init)

        wA = {n: ar.alloc([128, 8, 1024], BF16) for n in wA_names}
        ffn_base = 0
        x_t = ar.alloc([128, 8, T], F32)
        xb2 = ar.alloc([128, 2, T], BF16)
        sqb = ar.alloc([128, 2, T], BF16)
        rows = ar.alloc([1, 7, T], F32, parts=1)
        bc_t = ar.alloc([128, 6, T], F32)
        xn_t = ar.alloc([128, 8, T], F32)
        xn1_bf = ar.alloc([128, 8, T], BF16)
        t512 = ar.alloc([128, T], F32)
        ew_t = ar.alloc([128, 2, T], F32)
        e_t = ar.alloc([128, 2, T], F32)
        km_t = ar.alloc([128, 2, T], F32)
        num_t = ar.alloc([128, 2, T], F32)
        den_t = ar.alloc([128, T], F32)
        dd_t = ar.alloc([128, 2, T], F32)
        rn_t = ar.alloc([128, 2, T], F32)
        q_t = ar.alloc([128, T], F32)
        rw_bf = ar.alloc([128, 8, T], BF16)
        x2s_bf = ar.alloc([128, 2, T], BF16)
        amark = ar.mark_()

        def rowv(i):
            return rows[:, i, :]

        def bcv(i):
            return bc_t[:, i, :].rearrange("p (o t) -> p o t", o=1) \
                .broadcast_to([128, 8, T])

        if "A" in phases:
            def g_wA(h, mark):
                for n in wA_names:
                    mark(h.dma_start(out=wA[n][:], in_=wA_h[n]), 16)
            m["g_wA"] = em.step("G", body=g_wA)

            worder = ("wr", "ww", "wk", "wv")
            names = ("zrmm", "zwmm", "kmm", "vmm")

            def emit_sqF(g, dc):
                def v_st(h, mark, dc=dc, g=g):
                    h.tensor_copy(out=xb2[:, dc % 2, :], in_=x_t[:, dc, :])
                    mark(h.tensor_tensor(out=sqb[:, dc % 2, :],
                                         in0=xb2[:, dc % 2, :],
                                         in1=xb2[:, dc % 2, :], op=OP.mult))
                w = [("D", m[f"xld{g}_{dc}"])]
                if dc >= 2:
                    w.append(("T", m[f"st1mm{g}_{dc-2}"]))
                elif g == 1:
                    w.append(("T", m[f"st1mm0_{6+dc}"]))
                elif g > 1:
                    w.append(("T", m[f"st2mm{g-2}_{6+dc}"]))
                m[f"sqF{g}_{dc}"] = em.step("V", waits=w, body=v_st)

            def emit_st1(g, dc):
                def t_st(h, mark, dc=dc):
                    h.matmul(S1a, ones_bf[:], xb2[:, dc % 2, :],
                             start=(dc == 0), stop=(dc == 7))
                    mark(h.matmul(S2a, ones_bf[:], sqb[:, dc % 2, :],
                                  start=(dc == 0), stop=(dc == 7)))
                w = [("V", m[f"sqF{g}_{dc}"]), ("G", m["g_wA"]),
                     ("V", m["v_init"])]
                if dc == 0 and g > 0:
                    w += [("V", m[f"r1aF{g-1}"]), ("V", m[f"u{g-1}_7"])]
                m[f"st1mm{g}_{dc}"] = em.step("T", waits=w, body=t_st)

            def emit_FE_rows(g):
                def v_r1(h, mark):
                    h.tensor_scalar_mul(out=rowv(0), in0=S1a, scalar1=1.0 / 1024)
                    h.tensor_scalar(out=rowv(1), in0=S2a, scalar1=1.0 / 1024,
                                    scalar2=LN_EPS, op0=OP.mult, op1=OP.add)
                    h.tensor_tensor(out=rowv(2), in0=rowv(0), in1=rowv(0),
                                    op=OP.mult)
                    h.tensor_tensor(out=rowv(3), in0=rowv(1), in1=rowv(2),
                                    op=OP.subtract)
                    mark(h.tensor_scalar(out=rowv(2), in0=rowv(3),
                                         scalar1=1.0 + LN_EPS, scalar2=-LN_EPS,
                                         op0=OP.mult, op1=OP.add))
                w = [("T", m[f"st1mm{g}_7"])]
                if g == 1:
                    w.append(("T", m["bcFmm0_3"]))
                elif g > 1:
                    w.append(("T", m[f"bc2mm{g-2}_1"]))
                m[f"r1aF{g}"] = em.step("V", waits=w, body=v_r1)

                def s_r1(h, mark):
                    h.activation(out=rowv(4), in_=rowv(3), func=AF.Ln)
                    h.activation(out=rowv(5), in_=rowv(4), func=AF.Exp,
                                 scale=-0.5)
                    h.activation(out=rowv(6), in_=rowv(2), func=AF.Ln)
                    mark(h.activation(out=rowv(4), in_=rowv(6), func=AF.Exp,
                                      scale=-0.5))
                m[f"sr2F{g}"] = em.step("S", waits=[("V", m[f"r1aF{g}"])],
                                        body=s_r1)

                def v_r3(h, mark):
                    h.scalar_tensor_tensor(out=rowv(2), in0=rowv(0),
                                           scalar=-1.0, in1=rowv(4),
                                           op0=OP.mult, op1=OP.mult)
                    mark(h.scalar_tensor_tensor(out=rowv(1), in0=rowv(0),
                                                scalar=-1.0, in1=rowv(5),
                                                op0=OP.mult, op1=OP.mult))
                m[f"r1bF{g}"] = em.step("V", waits=[("S", m[f"sr2F{g}"])],
                                        body=v_r3)

            def emit_FE_bc(g, part):
                srcs = (4, 2, 5, 1)
                for i in (part * 2, part * 2 + 1):
                    sl = srcs[i]
                    bk = 6 + (i % 2)
                    w = [("V", m[f"r1bF{g}"])]
                    if i >= 2:
                        w.append(("S", m[f"cpF{g}_{i-2}"]))
                    m[f"bcFmm{g}_{i}"] = em.step(
                        "T", waits=w, body=lambda h, mark, bk=bk, sl=sl: mark(
                            h.matmul(bank(bk), ones_rf[:], rowv(sl),
                                     start=True, stop=True)))
                    w = [("T", m[f"bcFmm{g}_{i}"])]
                    if i < 2 and g > 0:
                        w.append(("V", m[f"xn1{g-1}_7"]))
                    if i >= 2 and g > 0:
                        w.append(("V", m[f"xn{g-1}_7"]))
                    m[f"cpF{g}_{i}"] = em.step(
                        "S", waits=w, body=lambda h, mark, bk=bk, i=i: mark(
                            h.copy(out=bc_t[:, i, :], in_=bank(bk))))

            def emit_bc2(g):
                m[f"bc2mm{g}_0"] = em.step(
                    "T", waits=[("V", m[f"r1b2L{g}"])],
                    body=lambda h, mark, g=g: mark(
                        h.matmul(bank(4), ones_rf[:], rowv(5),
                                 start=True, stop=True)))
                m[f"bc2mm{g}_1"] = em.step(
                    "T", body=lambda h, mark, g=g: mark(
                        h.matmul(bank(5), ones_rf[:], rowv(1),
                                 start=True, stop=True)))
                w = [("T", m[f"bc2mm{g}_0"])]
                if g > 0:
                    w.append(("G", m[f"x2g{g-1}_7"]))
                m[f"cp2a{g}"] = em.step(
                    "S", waits=w, body=lambda h, mark, g=g: mark(
                        h.copy(out=bc_t[:, 4, :], in_=bank(4))))
                m[f"cp2b{g}"] = em.step(
                    "S", waits=[("T", m[f"bc2mm{g}_1"])],
                    body=lambda h, mark, g=g: mark(
                        h.copy(out=bc_t[:, 5, :], in_=bank(5))))

            def emit_xn1(g, dc):
                def v_x1(h, mark, dc=dc):
                    h.tensor_tensor(out=q_t[:], in0=x_t[:, dc, :],
                                    in1=bc_t[:, 0, :], op=OP.mult)
                    h.tensor_tensor(out=q_t[:], in0=q_t[:],
                                    in1=bc_t[:, 1, :], op=OP.add)
                    mark(h.tensor_scalar(out=xn1_bf[:, dc, :], in0=q_t[:],
                                         scalar1=g1_t[:, dc:dc + 1],
                                         scalar2=b1_t[:, dc:dc + 1],
                                         op0=OP.mult, op1=OP.add))
                m[f"xn1{g}_{dc}"] = em.step(
                    "V", waits=[("S", m[f"cpF{g}_1"])], body=v_x1)

            def d_xld_full(g):
                def d_x(h, mark, g=g):
                    mk = mark(h.dma_start(
                        out=x_t[:], in_=x_h[:, :, g * T:(g + 1) * T]), 16)
                    for dc in range(8):
                        m[f"xld{g}_{dc}"] = mk
                w = []
                if g > 0:
                    w = [("V", m[f"xn{g-1}_7"]), ("V", m[f"xn1{g-1}_7"])]
                em.step("D", waits=w, body=d_x)

            if "F" in phases:
                _sv = ar.mark_()
                ar.reset(ffn_base)
                wgb = ar.alloc([128, 8, 1536], BF16)
                wub = ar.alloc([128, 8, 1536], BF16)
                wdb = ar.alloc([128, 12, 1024], BF16)
                ar.reset(_sv)

                def g_wf(hf):
                    hs = slice(hf * 1536, (hf + 1) * 1536)
                    hcs = slice(hf * 12, (hf + 1) * 12)
                    w = [("T", m[f"vmm{NG-1}_7"])] if hf == 0 else \
                        [("T", m[f"gu0_{NG-1}_11"])]
                    m[f"gwf{hf}a"] = em.step(
                        "G", waits=w, body=lambda h, mark, hs=hs: (
                            mark(h.dma_start(out=wgb[:], in_=wg_h[:, :, hs]),
                                 16),
                            mark(h.dma_start(out=wub[:], in_=wu_h[:, :, hs]),
                                 16))[-1])
                    w = [("T", m[f"tm{NG-1}_7"])] if hf == 0 else \
                        [("T", m[f"dn0_{NG-1}_7"])]
                    m[f"gwf{hf}"] = em.step(
                        "G", waits=w, body=lambda h, mark, hcs=hcs: mark(
                            h.dma_start(out=wdb[:], in_=wd_h[:, hcs, :]), 16))

            d_xld_full(0)
            for dc in range(8):
                emit_sqF(0, dc)
                emit_st1(0, dc)
            emit_FE_rows(0)
            emit_FE_bc(0, 0)
            emit_FE_bc(0, 1)
            for dc in range(8):
                emit_xn1(0, dc)

            def v_xn0(h, mark):
                h.tensor_tensor(out=xn_t[:], in0=x_t[:], in1=bcv(2),
                                op=OP.mult)
                mk = mark(h.tensor_tensor(out=xn_t[:], in0=xn_t[:],
                                          in1=bcv(3), op=OP.add))
                for dc in range(8):
                    m[f"xn0_{dc}"] = mk
            em.step("V", waits=[("S", m["cpF0_3"])], body=v_xn0)

            def g_xnsp0(h, mark):
                mk = mark(h.dma_start(out=xn_sp[:, :, 0:T], in_=xn_t[:]), 16)
                for dc in range(8):
                    m[f"xnsp0_{dc}"] = mk
            em.step("G", waits=[("V", m["xn0_7"])], body=g_xnsp0)
            d_xld_full(1)

            for g in range(NG):
                for ec in range(8):
                    base = 0 if ec % 2 == 0 else 4
                    pr = ec % 2
                    w = [("V", m[f"xn1{g}_7"])]
                    if ec == 0 and g > 0:
                        w.append(("V", m[f"r1cL{g-1}"]))
                    if ec == 1:
                        w.append(("S", m[f"cpF{g}_3"]))
                        if g > 0:
                            w.append(("S", m[f"cp2b{g-1}"]))
                    if ec >= 2:
                        w += [("V", m[f"u{g}_{ec-2}"]),
                              ("S", m[f"rn{g}_{ec-2}"])]
                    def t_proj(h, mark, ec=ec, base=base, g=g):
                        for wi, n in enumerate(worder):
                            ins = None
                            for dc in range(8):
                                ins = h.matmul(bank(base + wi),
                                               wA[n][:, dc,
                                                     ec * 128:(ec + 1) * 128],
                                               xn1_bf[:, dc, :],
                                               start=(dc == 0), stop=(dc == 7))
                            m[f"{names[wi]}{g}_{ec}"] = mark(ins)
                    em.step("T", waits=w, body=t_proj)
                    if ec == 0 and g > 0:
                        emit_bc2(g - 1)

                    w = [("T", m[f"zwmm{g}_{ec}"])]
                    if ec >= 2:
                        w.append(("V", m[f"km{g}_{ec-2}"]))
                    elif g > 0:
                        w.append(("V", m[f"km{g-1}_{ec+6}"]))
                    m[f"ew{g}_{ec}"] = em.step(
                        "S", waits=w, body=lambda h, mark, base=base, pr=pr: mark(
                            h.activation(out=ew_t[:, pr, :], in_=bank(base + 1),
                                         func=AF.Exp)))
                    if ec >= 1:
                        m[f"rcp{g}_{ec-1}"] = em.step(
                            "S", waits=[("V", m[f"vdd{g}_{ec-1}"])],
                            body=lambda h, mark, pr=1 - pr: (
                                h.activation(out=dd_t[:, pr, :],
                                             in_=dd_t[:, pr, :], func=AF.Ln),
                                mark(h.activation(out=dd_t[:, pr, :],
                                                  in_=dd_t[:, pr, :],
                                                  func=AF.Exp,
                                                  scale=-1.0)))[-1])
                    m[f"km{g}_{ec}"] = em.step(
                        "V", waits=[("T", m[f"kmm{g}_{ec}"]),
                                    ("S", m[f"ew{g}_{ec}"])],
                        body=lambda h, mark, base=base, pr=pr: mark(
                            h.tensor_tensor(out=km_t[:, pr, :],
                                            in0=bank(base + 2),
                                            in1=ew_t[:, pr, :],
                                            op=OP.subtract)))
                    w = [("V", m[f"km{g}_{ec}"])]
                    if ec >= 2:
                        w.append(("V", m[f"sce{g}_{ec-2}"]))
                    elif g > 0:
                        w.append(("V", m[f"sce{g-1}_{ec+6}"]))
                    m[f"e{g}_{ec}"] = em.step(
                        "S", waits=w, body=lambda h, mark, pr=pr: mark(
                            h.activation(out=e_t[:, pr, :], in_=km_t[:, pr, :],
                                         func=AF.Exp)))
                    w = [("T", m[f"zrmm{g}_{ec}"])]
                    if ec >= 2:
                        w.append(("V", m[f"vdd{g}_{ec-2}"]))
                    elif g > 0:
                        w.append(("V", m[f"vdd{g-1}_{ec+6}"]))
                    def s_rn(h, mark, base=base, pr=pr):
                        h.activation(out=rn_t[:, pr, :], in_=bank(base + 0),
                                     func=AF.Exp, scale=-1.0)
                        mark(h.add(out=rn_t[:, pr, :], in_=rn_t[:, pr, :],
                                   add=1.0))
                    m[f"rn{g}_{ec}"] = em.step("S", waits=w, body=s_rn)

                    def v_u(h, mark, ec=ec, base=base, g=g, pr=pr):
                        u = h.tensor_tensor(out=km_t[:, pr, :],
                                            in0=e_t[:, pr, :],
                                            in1=bank(base + 3), op=OP.mult)
                        m[f"u{g}_{ec}"] = mark(u)
                    em.step("V", waits=[("S", m[f"e{g}_{ec}"]),
                                        ("T", m[f"vmm{g}_{ec}"]),
                                        ("S", m["s_dec"])], body=v_u)

                    if ec >= 1:
                        m[f"rw{g}_{ec-1}"] = em.step(
                            "V", waits=[("S", m[f"rcp{g}_{ec-1}"])],
                            body=lambda h, mark, ec=ec, pr=1 - pr: mark(
                                h.tensor_tensor(out=rw_bf[:, ec - 1, :],
                                                in0=num_t[:, pr, :],
                                                in1=dd_t[:, pr, :],
                                                op=OP.mult)))

                    def v_scan(h, mark, ec=ec, g=g, pr=pr):
                        dcol = dec_t[:, ec:ec + 1].broadcast_to([128, T])
                        h.tensor_tensor_scan(out=num_t[:, pr, :], data0=dcol,
                                             data1=km_t[:, pr, :],
                                             initial=cn_t[:, ec:ec + 1],
                                             op0=OP.mult, op1=OP.add)
                        h.tensor_tensor_scan(out=den_t[:], data0=dcol,
                                             data1=e_t[:, pr, :],
                                             initial=cd_t[:, ec:ec + 1],
                                             op0=OP.mult, op1=OP.add)
                        h.tensor_copy(out=cn_t[:, ec:ec + 1],
                                      in_=num_t[:, pr, T - 1:T])
                        h.engine_nop()
                        mark(h.tensor_copy(out=cd_t[:, ec:ec + 1],
                                           in_=den_t[:, T - 1:T]))
                    w = []
                    if ec >= 2:
                        w.append(("S", m[f"rcp{g}_{ec-2}"]))
                    elif g > 0:
                        w.append(("S", m[f"rcp{g-1}_{ec+6}"]))
                    m[f"sce{g}_{ec}"] = em.step("V", waits=w, body=v_scan)

                    m[f"vdd{g}_{ec}"] = em.step(
                        "V", waits=[("S", m[f"rn{g}_{ec}"])],
                        body=lambda h, mark, pr=pr: mark(
                            h.scalar_tensor_tensor(out=dd_t[:, pr, :],
                                                   in0=den_t[:],
                                                   scalar=WKV_EPS,
                                                   in1=rn_t[:, pr, :],
                                                   op0=OP.add, op1=OP.mult)))

                    if g > 0:
                        def g_x2(h, mark, dc=ec, g=g):
                            h.tensor_tensor(out=t512[:],
                                            in0=xn_t[:, dc, :],
                                            in1=bc_t[:, 4, :], op=OP.mult)
                            mark(h.tensor_tensor(out=t512[:], in0=t512[:],
                                                 in1=bc_t[:, 5, :], op=OP.add))
                        w = [("S", m[f"cp2b{g-1}"]), ("S", m[f"cp2a{g-1}"])]
                        if ec >= 1:
                            w.append(("V", m[f"x2v{g-1}_{ec-1}"]))
                        elif g > 1:
                            w.append(("V", m[f"x2v{g-2}_7"]))
                        m[f"x2g{g-1}_{ec}"] = em.step("G", waits=w, body=g_x2)

                        def v_x2v(h, mark, dc=ec):
                            mark(h.tensor_scalar(out=x2s_bf[:, dc % 2, :],
                                                 in0=t512[:],
                                                 scalar1=g2_t[:, dc:dc + 1],
                                                 scalar2=b2_t[:, dc:dc + 1],
                                                 op0=OP.mult, op1=OP.add))
                        w = [("G", m[f"x2g{g-1}_{ec}"])]
                        if ec >= 2:
                            w.append(("D", m[f"xsp2{g-1}_{ec-2}"]))
                        elif g > 1:
                            w.append(("D", m[f"xsp2{g-2}_{6+ec}"]))
                        m[f"x2v{g-1}_{ec}"] = em.step("V", waits=w, body=v_x2v)
                        m[f"xsp2{g-1}_{ec}"] = em.step(
                            "D", waits=[("V", m[f"x2v{g-1}_{ec}"])],
                            body=lambda h, mark, g=g, dc=ec: mark(
                                h.dma_start(
                                    out=xn2_sp[:, dc, (g - 1) * T:g * T],
                                    in_=x2s_bf[:, dc % 2, :]), 16))

                        def v_xn(h, mark, dc=ec, g=g):
                            h.tensor_tensor(out=xn_t[:, dc, :],
                                            in0=x_t[:, dc, :],
                                            in1=bc_t[:, 2, :], op=OP.mult)
                            mark(h.tensor_tensor(out=xn_t[:, dc, :],
                                                 in0=xn_t[:, dc, :],
                                                 in1=bc_t[:, 3, :], op=OP.add))
                        w = [("S", m[f"cpF{g}_3"]), ("G", m[f"x2g{g-1}_{ec}"]),
                             ("D", m[f"ysp{g-1}"])]
                        m[f"xn{g}_{ec}"] = em.step("V", waits=w, body=v_xn)
                        m[f"xnsp{g}_{ec}"] = em.step(
                            "G", waits=[("V", m[f"xn{g}_{ec}"])],
                            body=lambda h, mark, g=g, dc=ec: mark(
                                h.dma_start(
                                    out=xn_sp[:, dc, g * T:(g + 1) * T],
                                    in_=xn_t[:, dc, :]), 16))
                        if g + 1 < NG:
                            m[f"xld{g+1}_{ec}"] = em.step(
                                "D", waits=[("V", m[f"xn{g}_{ec}"]),
                                            ("V", m[f"xn1{g}_7"])],
                                body=lambda h, mark, g=g, dc=ec: mark(
                                    h.dma_start(
                                        out=x_t[:, dc, :],
                                        in_=x_h[:, dc,
                                                (g + 1) * T:(g + 2) * T]), 16))

                def s_rcp7(h, mark):
                    h.activation(out=dd_t[:, 1, :], in_=dd_t[:, 1, :],
                                 func=AF.Ln)
                    mark(h.activation(out=dd_t[:, 1, :], in_=dd_t[:, 1, :],
                                      func=AF.Exp, scale=-1.0))
                m[f"rcp{g}_7"] = em.step(
                    "S", waits=[("V", m[f"vdd{g}_7"])], body=s_rcp7)

                if g + 1 < NG:
                    emit_sqF(g + 1, 0)
                    emit_sqF(g + 1, 1)
                m[f"rw{g}_7"] = em.step(
                    "V", waits=[("S", m[f"rcp{g}_7"])],
                    body=lambda h, mark: mark(
                        h.tensor_tensor(out=rw_bf[:, 7, :],
                                        in0=num_t[:, 1, :],
                                        in1=dd_t[:, 1, :], op=OP.mult)))
                if g + 1 < NG:
                    emit_st1(g + 1, 0)
                    emit_st1(g + 1, 1)
                    for dc in range(2, 8):
                        emit_sqF(g + 1, dc)
                        emit_st1(g + 1, dc)
                    emit_FE_rows(g + 1)

                def emit_sqL(g, dc):
                    def s_st2(h, mark, dc=dc):
                        h.activation(out=xb2[:, dc % 2, :], in_=xn_t[:, dc, :],
                                     func=AF.Copy)
                        mark(h.activation(out=sqb[:, dc % 2, :],
                                          in_=xn_t[:, dc, :], func=AF.Square))
                    w = [("V", m[f"y{g}_{dc}"])]
                    if dc >= 2:
                        w.append(("T", m[f"st2mm{g}_{dc-2}"]))
                    else:
                        w.append(("T", m[f"st1mm{g+1}_{6+dc}"] if g + 1 < NG
                                  else m[f"st1mm{g}_{6+dc}"]))
                    m[f"sqL{g}_{dc}"] = em.step("S", waits=w, body=s_st2)

                def emit_st2(g, dc):
                    def t_st2(h, mark, dc=dc):
                        h.matmul(S1b, ones_bf[:], xb2[:, dc % 2, :],
                                 start=(dc == 0), stop=(dc == 7))
                        mark(h.matmul(S2b, ones_bf[:], sqb[:, dc % 2, :],
                                      start=(dc == 0), stop=(dc == 7)))
                    w = [("S", m[f"sqL{g}_{dc}"])]
                    if dc == 0 and g > 0:
                        w.append(("V", m[f"r1cL{g-1}"]))
                    m[f"st2mm{g}_{dc}"] = em.step("T", waits=w, body=t_st2)

                for oc in range(8):
                    w = [("V", m[f"rw{g}_7"])]
                    if oc >= 2:
                        w.append(("V", m[f"y{g}_{oc-2}"]))
                    def t_wo(h, mark, oc=oc, g=g):
                        ins = None
                        for dc in range(8):
                            ins = h.matmul(bank(oc % 2),
                                           wA["wo"][:, dc,
                                                    oc * 128:(oc + 1) * 128],
                                           rw_bf[:, dc, :],
                                           start=(dc == 0), stop=(dc == 7))
                        m[f"tm{g}_{oc}"] = mark(ins)
                    em.step("T", waits=w, body=t_wo)
                    m[f"y{g}_{oc}"] = em.step(
                        "V", waits=[("T", m[f"tm{g}_{oc}"]),
                                    ("G", m[f"xnsp{g}_{oc}"])],
                        body=lambda h, mark, oc=oc: mark(
                            h.tensor_tensor(out=xn_t[:, oc, :],
                                            in0=bank(oc % 2),
                                            in1=xn_t[:, oc, :], op=OP.add)))
                    if oc == 1 and g + 1 < NG:
                        emit_FE_bc(g + 1, 0)
                    if oc >= 4:
                        emit_sqL(g, oc - 4)
                    if oc >= 2 and g + 1 < NG:
                        emit_xn1(g + 1, oc - 2)
                    if oc == 3 and g + 1 < NG:
                        emit_FE_bc(g + 1, 1)
                    if oc >= 4:
                        emit_st2(g, oc - 4)

                for dc in range(4, 8):
                    emit_sqL(g, dc)
                    if g + 1 < NG and dc < 6:
                        emit_xn1(g + 1, dc + 2)
                    emit_st2(g, dc)

                m[f"ysp{g}"] = em.step(
                    "D", waits=[("V", m[f"y{g}_7"])],
                    body=lambda h, mark, g=g: mark(
                        h.dma_start(out=y_sp[:, :, g * T:(g + 1) * T],
                                    in_=xn_t[:]), 16))
                if g == NG - 1 and "F" in phases:
                    g_wf(0)

                def v_r1b(h, mark):
                    h.tensor_scalar_mul(out=rowv(0), in0=S1b, scalar1=1.0 / 1024)
                    h.tensor_scalar(out=rowv(1), in0=S2b, scalar1=1.0 / 1024,
                                    scalar2=LN_EPS, op0=OP.mult, op1=OP.add)
                    h.tensor_tensor(out=rowv(2), in0=rowv(0), in1=rowv(0),
                                    op=OP.mult)
                    mark(h.tensor_tensor(out=rowv(3), in0=rowv(1), in1=rowv(2),
                                         op=OP.subtract))
                w = [("T", m[f"st2mm{g}_7"])]
                if g + 1 < NG:
                    w.append(("T", m[f"bcFmm{g+1}_3"]))
                m[f"r1cL{g}"] = em.step("V", waits=w, body=v_r1b)

                def s_r3(h, mark):
                    h.activation(out=rowv(4), in_=rowv(3), func=AF.Ln)
                    mark(h.activation(out=rowv(5), in_=rowv(4), func=AF.Exp,
                                      scale=-0.5))
                m[f"sr3L{g}"] = em.step("S", waits=[("V", m[f"r1cL{g}"])],
                                        body=s_r3)

                def v_r4(h, mark):
                    mark(h.scalar_tensor_tensor(out=rowv(1), in0=rowv(0),
                                                scalar=-1.0, in1=rowv(5),
                                                op0=OP.mult, op1=OP.mult))
                m[f"r1b2L{g}"] = em.step("V", waits=[("S", m[f"sr3L{g}"])],
                                         body=v_r4)

                if g == NG - 1:
                    emit_bc2(g)

            g = NG - 1
            for dc in range(8):
                def g_x2e(h, mark, dc=dc, g=g):
                    h.tensor_tensor(out=t512[:], in0=xn_t[:, dc, :],
                                    in1=bc_t[:, 4, :], op=OP.mult)
                    mark(h.tensor_tensor(out=t512[:], in0=t512[:],
                                         in1=bc_t[:, 5, :], op=OP.add))
                w = [("S", m[f"cp2b{g}"]), ("S", m[f"cp2a{g}"])]
                if dc >= 1:
                    w.append(("V", m[f"x2v{g}_{dc-1}"]))
                else:
                    w.append(("V", m[f"x2v{g-1}_7"]))
                m[f"x2g{g}_{dc}"] = em.step("V", waits=w, body=g_x2e)

                def v_x2v(h, mark, dc=dc):
                    mark(h.tensor_scalar(out=x2s_bf[:, dc % 2, :],
                                         in0=t512[:],
                                         scalar1=g2_t[:, dc:dc + 1],
                                         scalar2=b2_t[:, dc:dc + 1],
                                         op0=OP.mult, op1=OP.add))
                w = [("G", m[f"x2g{g}_{dc}"])]
                if dc >= 2:
                    w.append(("D", m[f"xsp2{g}_{dc-2}"]))
                else:
                    w.append(("D", m[f"xsp2{g-1}_{6+dc}"]))
                m[f"x2v{g}_{dc}"] = em.step("V", waits=w, body=v_x2v)
                m[f"xsp2{g}_{dc}"] = em.step(
                    "D", waits=[("V", m[f"x2v{g}_{dc}"])],
                    body=lambda h, mark, g=g, dc=dc: mark(
                        h.dma_start(out=xn2_sp[:, dc, g * T:(g + 1) * T],
                                    in_=x2s_bf[:, dc % 2, :]), 16))

        snapA = em.snap()

        if "F" in phases:
            ar.reset(ffn_base)
            ar.alloc([128, 8, 1536], BF16)
            ar.alloc([128, 8, 1536], BF16)
            ar.alloc([128, 12, 1024], BF16)
            xnb_t = ar.alloc([128, 2, 8, T], BF16)
            yF_t = ar.alloc([128, 2, 8, T], F32)
            xF_t = ar.alloc([128, 2, 8, T], F32)
            h_bf = ar.alloc([128, 12, T], BF16)
            sg_t = ar.alloc([128, 2, T], F32)

            first = {"V": True, "S": True}

            def bw(eng, w):
                if first.get(eng):
                    first[eng] = False
                    return list(w) + [("T", snapA["T"])]
                return list(w)

            fxn_q = {}

            def f_loads(hf, g):
                par = g % 2
                src_sp = y_sp if hf == 0 else acc_sp
                w = []
                w2 = []
                xq = "D"
                pg, ph = (g - 2, hf) if g >= 2 else (NG - 2 + g, hf - 1)
                if pg >= 0 and ph >= 0:
                    w += [("T", m[f"gu{ph}_{pg}_11"])]
                    w2 += [("D", m[f"fsp{ph}_{pg}"])]
                    if hf == 1 and g < 2:
                        xq = "S"
                elif hf == 0 and g < 2:
                    xq = "S"
                    w += [("T", m[f"tm{NG-1}_7"]), ("V", m[f"xn1{NG-1}_7"]),
                          ("V", m[f"xn{NG-1}_7"]),
                          ("G", m[f"xnsp{NG-1}_7"])]
                    w2 += [(o, snapA[o]) for o in ENGS]
                def d_xnb(h, mark, g=g, par=par):
                    gs = slice(g * T, (g + 1) * T)
                    mark(h.dma_start(out=xnb_t[:, par], in_=xn_sp[:, :, gs]), 16)
                fxn_q[(hf, g)] = xq
                m[f"fxn{hf}_{g}"] = em.step(xq, waits=w, body=d_xnb)
                def d_yx(h, mark, g=g, par=par, src_sp=src_sp, hf=hf):
                    gs = slice(g * T, (g + 1) * T)
                    mark(h.dma_start(out=yF_t[:, par], in_=src_sp[:, :, gs]), 16)
                    if hf == 0:
                        mark(h.dma_start(out=xF_t[:, par], in_=x_h[:, :, gs]),
                             16)
                m[f"fxl{hf}_{g}"] = em.step("D", waits=w2, body=d_yx)

            hlast = None
            for hf in range(2):
                if hf > 0:
                    g_wf(hf)
                f_loads(hf, 0)
                f_loads(hf, 1)
                for g in range(NG):
                    par = g % 2
                    for hc in range(12):
                        gb, ub = (hc % 2) * 2, (hc % 2) * 2 + 1
                        w = [(fxn_q[(hf, g)], m[f"fxn{hf}_{g}"]),
                             ("G", m[f"gwf{hf}a"])]
                        if hc >= 2:
                            w.append(("V", m[f"hh{hf}_{g}_{hc-2}"]))
                        elif g > 0 or hf > 0:
                            pg, ph = (g - 1, hf) if g > 0 else (NG - 1, hf - 1)
                            w.append(("V", m[f"fev{ph}_{pg}_{ub}"]))
                        elif hf == 0 and g == 0:
                            w += [("V", m[f"y{NG-1}_7"]),
                                  ("V", m[f"r1cL{NG-1}"]),
                                  ("S", m[f"cp2b{NG-1}"])]
                        def t_gu(h, mark, hc=hc, gb=gb, ub=ub, hf=hf, g=g,
                                 par=par):
                            for dc in range(8):
                                h.matmul(bank(gb),
                                         wgb[:, dc, hc * 128:(hc + 1) * 128],
                                         xnb_t[:, par, dc, :],
                                         start=(dc == 0), stop=(dc == 7))
                            ins = None
                            for dc in range(8):
                                ins = h.matmul(bank(ub),
                                               wub[:, dc,
                                                   hc * 128:(hc + 1) * 128],
                                               xnb_t[:, par, dc, :],
                                               start=(dc == 0), stop=(dc == 7))
                            m[f"gu{hf}_{g}_{hc}"] = mark(ins)
                        em.step("T", waits=w, body=t_gu)

                        w = [("T", m[f"gu{hf}_{g}_{hc}"])]
                        if hc >= 2:
                            w.append(("V", m[f"hh{hf}_{g}_{hc-2}"]))
                        m[f"sg{hf}_{g}_{hc}"] = em.step(
                            "S", waits=bw("S", w),
                            body=lambda h, mark, hc=hc, gb=gb: mark(
                                h.activation(out=sg_t[:, hc % 2, :],
                                             in_=bank(gb), func=AF.Silu,
                                             scale=QS)))
                        m[f"hh{hf}_{g}_{hc}"] = em.step(
                            "V", waits=bw("V", [("S", m[f"sg{hf}_{g}_{hc}"])]),
                            body=lambda h, mark, hc=hc, ub=ub: mark(
                                h.scalar_tensor_tensor(
                                    out=h_bf[:, hc, :], in0=bank(ub),
                                    scalar=QS, in1=sg_t[:, hc % 2, :],
                                    op0=OP.mult, op1=OP.mult)))

                    for dc in range(8):
                        w = [("V", m[f"hh{hf}_{g}_11"]), ("G", m[f"gwf{hf}"])]
                        if g > 0 or hf > 0:
                            pg, ph = (g - 1, hf) if g > 0 else (NG - 1, hf - 1)
                            w.append(("V", m[f"fev{ph}_{pg}_{dc}"]))
                        def t_dn(h, mark, dc=dc, hf=hf, g=g):
                            ins = None
                            for hc in range(12):
                                ins = h.matmul(bank(dc),
                                               wdb[:, hc,
                                                   dc * 128:(dc + 1) * 128],
                                               h_bf[:, hc, :],
                                               start=(hc == 0), stop=(hc == 11))
                            m[f"dn{hf}_{g}_{dc}"] = mark(ins)
                        em.step("T", waits=w, body=t_dn)

                        def v_fev(h, mark, dc=dc, par=par, hf=hf, g=g):
                            stt = h.scalar_tensor_tensor(
                                out=yF_t[:, par, dc, :], in0=bank(dc),
                                scalar=QS, in1=yF_t[:, par, dc, :],
                                op0=OP.mult, op1=OP.add)
                            if hf == 0:
                                m[f"fev{hf}_{g}_{dc}"] = mark(
                                    h.tensor_tensor(out=yF_t[:, par, dc, :],
                                                    in0=yF_t[:, par, dc, :],
                                                    in1=xF_t[:, par, dc, :],
                                                    op=OP.add))
                            else:
                                m[f"fev{hf}_{g}_{dc}"] = mark(stt)
                        em.step("V", waits=[("T", m[f"dn{hf}_{g}_{dc}"]),
                                            ("D", m[f"fxl{hf}_{g}"])],
                                body=v_fev)

                    dst_sp = acc_sp
                    m[f"fsp{hf}_{g}"] = em.step(
                        "D", waits=[("V", m[f"fev{hf}_{g}_7"])],
                        body=lambda h, mark, g=g, par=par: mark(
                            h.dma_start(out=acc_sp[:, :, g * T:(g + 1) * T],
                                        in_=yF_t[:, par]), 16))
                    if g + 2 < NG:
                        f_loads(hf, g + 2)
                    elif hf == 0 and g + 2 < NG + 2:
                        pass

        snapF = em.snap()

        if "C" in phases:
            ar.reset(ffn_base)
            wkb = ar.alloc([128, 8, 2048], BF16)
            wvb = ar.alloc([128, 16, 1024], BF16)
            wcb = ar.alloc([128, 8, 1024], BF16)
            xn2_t = ar.alloc([128, 2, 8, T], BF16)
            kk_bf = ar.alloc([128, 16, T], BF16)
            cv_t = ar.alloc([128, 2, 8, T], F32)
            accC_t = ar.alloc([128, 2, 8, T], F32)
            rl_t = ar.alloc([128, 2, T], F32)
            rr2_t = ar.alloc([128, T], F32)
            tc_t = ar.alloc([128, T], F32)

            firstC = {"T": True, "V": True, "S": True, "D": True}

            def bwC(eng, w):
                if firstC[eng]:
                    firstC[eng] = False
                    return list(w) + [(o, snapF[o]) for o in ENGS if o != eng]
                return list(w)

            def g_wc(cf):
                fcs = slice(cf * 16, (cf + 1) * 16)
                for j in range(4):
                    cs = slice(j * 512, (j + 1) * 512)
                    fs = slice(cf * 2048 + j * 512, cf * 2048 + (j + 1) * 512)
                    w = [("T", m[f"gu1_{NG-1}_11"])] if cf == 0 else \
                        [("T", m[f"kk0_{NG-1}_{4*j+3}"])]
                    m[f"gwck{cf}_{j}"] = em.step(
                        "G", waits=w, body=lambda h, mark, cs=cs, fs=fs: mark(
                            h.dma_start(out=wkb[:, :, cs],
                                        in_=wkey_h[:, :, fs]), 16))
                for j in range(4):
                    cs = slice(j * 256, (j + 1) * 256)
                    w = [("T", m[f"dn1_{NG-1}_7"])] if cf == 0 else \
                        [("T", m[f"cv0_{NG-1}_{2*j+1}"])]
                    def g_wv(h, mark, cs=cs, fcs=fcs, cf=cf, j=j):
                        mark(h.dma_start(out=wvb[:, :, cs],
                                         in_=wval_h[:, fcs, cs]), 16)
                        if cf == 0 and j == 3:
                            mark(h.dma_start(out=wcb[:], in_=wrec_h), 16)
                    m[f"gwcv{cf}_{j}"] = em.step("G", waits=w, body=g_wv)
                m[f"gwc{cf}"] = m[f"gwcv{cf}_3"]

            cxl_q = {}

            def c_loads(cf, g):
                par = g % 2
                w = []
                xq = "D"
                pg, pc = (g - 2, cf) if g >= 2 else (NG - 2 + g, cf - 1)
                if pg >= 0 and pc >= 0:
                    w.append(("D", m[f"csp{pc}_{pg}"] if pc == 0
                              else m[f"osp{pg}"]))
                    w.append(("T", m[f"kk{pc}_{pg}_15"] if pc == 0
                              else m[f"rr{pc}_{pg}_7"]))
                    if g < 2:
                        xq = "S"
                elif cf == 0 and g < 2:
                    xq = "S"
                    w.append(("D", m[f"fsp1_{NG-2+g}"]))
                def d_cl(h, mark, cf=cf, g=g, par=par):
                    gs = slice(g * T, (g + 1) * T)
                    if cf == 1:
                        mark(h.dma_start(out=cv_t[:, par],
                                         in_=cv_sp[:, :, gs]), 16)
                        mark(h.dma_start(out=accC_t[:, par],
                                         in_=acc_sp[:, :, gs]), 16)
                    mark(h.dma_start(out=xn2_t[:, par], in_=xn2_sp[:, :, gs]),
                         16)
                cxl_q[(cf, g)] = xq
                m[f"cxl{cf}_{g}"] = em.step(
                    xq, waits=bwC("D", w) if xq == "D" else w, body=d_cl)

            for cf in range(2):
                g_wc(cf)
                if cf == 0:
                    c_loads(cf, 0)
                    c_loads(cf, 1)
                else:
                    for fg in (0, 1):
                        def d_cvacc(h, mark, g=fg, par=fg % 2):
                            gs = slice(g * T, (g + 1) * T)
                            mark(h.dma_start(out=cv_t[:, par],
                                             in_=cv_sp[:, :, gs]), 16)
                            mark(h.dma_start(out=accC_t[:, par],
                                             in_=acc_sp[:, :, gs]), 16)
                        m[f"cxlb1_{fg}"] = em.step(
                            "D", waits=[("D", m[f"csp0_{NG-2+fg}"])],
                            body=d_cvacc)
                for g in range(NG):
                    par = g % 2
                    for fc in range(16):
                        w = [(cxl_q[(cf, g)], m[f"cxl{cf}_{g}"]),
                             ("G", m[f"gwck{cf}_{fc // 4}"])]
                        if fc >= 4:
                            w.append(("S", m[f"rl{cf}_{g}_{fc-4}"]))
                        elif g > 0 or cf > 0:
                            pg, pc = (g - 1, cf) if g > 0 else (NG - 1, cf - 1)
                            if pc == 0:
                                w.append(("V", m[f"cvev{pc}_{pg}_{fc}"]))
                            else:
                                w.append(("S", m[f"rr2{pc}_{pg}_{fc}"]))
                        else:
                            w.append(("V", m[f"fev1_{NG-1}_{fc}"]))
                        def t_kk(h, mark, fc=fc, cf=cf, g=g, par=par):
                            ins = None
                            for dc in range(8):
                                ins = h.matmul(bank(fc % 4),
                                               wkb[:, dc,
                                                   fc * 128:(fc + 1) * 128],
                                               xn2_t[:, par, dc, :],
                                               start=(dc == 0), stop=(dc == 7))
                            m[f"kk{cf}_{g}_{fc}"] = mark(ins)
                        em.step("T", waits=bwC("T", w), body=t_kk)
                        w = [("T", m[f"kk{cf}_{g}_{fc}"])]
                        if fc >= 2:
                            w.append(("V", m[f"kb{cf}_{g}_{fc-2}"]))
                        m[f"rl{cf}_{g}_{fc}"] = em.step(
                            "S", waits=bwC("S", w),
                            body=lambda h, mark, fc=fc: mark(
                                h.activation(out=rl_t[:, fc % 2, :],
                                             in_=bank(fc % 4), func=AF.Relu)))
                        m[f"kb{cf}_{g}_{fc}"] = em.step(
                            "V", waits=bwC("V", [("S", m[f"rl{cf}_{g}_{fc}"])]),
                            body=lambda h, mark, fc=fc: mark(
                                h.tensor_tensor(out=kk_bf[:, fc, :],
                                                in0=rl_t[:, fc % 2, :],
                                                in1=rl_t[:, fc % 2, :],
                                                op=OP.mult)))

                    for dc in range(8):
                        w = [("V", m[f"kb{cf}_{g}_15"]),
                             ("G", m[f"gwcv{cf}_{dc // 2}"])]
                        if cf == 0 and g > 0:
                            w.append(("V", m[f"cvev{cf}_{g-1}_{dc}"]))
                        if cf == 1:
                            w.append(("V", m[f"oev{cf}_{g-1}_{dc}"])
                                     if g > 0 else
                                     ("V", m[f"cvev0_{NG-1}_{dc}"]))
                        def t_cv(h, mark, dc=dc, cf=cf, g=g):
                            ins = None
                            for fc in range(16):
                                ins = h.matmul(bank(dc),
                                               wvb[:, fc,
                                                   dc * 128:(dc + 1) * 128],
                                               kk_bf[:, fc, :],
                                               start=(fc == 0), stop=(fc == 15))
                            m[f"cv{cf}_{g}_{dc}"] = mark(ins)
                        em.step("T", waits=w, body=t_cv)

                        if cf == 0:
                            m[f"cvev{cf}_{g}_{dc}"] = em.step(
                                "V", waits=[("T", m[f"cv{cf}_{g}_{dc}"])],
                                body=lambda h, mark, dc=dc, par=par: mark(
                                    h.tensor_copy(out=cv_t[:, par, dc, :],
                                                  in_=bank(dc))))
                        else:
                            m[f"cvt{cf}_{g}_{dc}"] = em.step(
                                "V", waits=[("T", m[f"cv{cf}_{g}_{dc}"]),
                                            ("D", m[f"cxlb1_{g}"]) if g < 2
                                            else (cxl_q[(cf, g)],
                                                  m[f"cxl{cf}_{g}"])],
                                body=lambda h, mark, dc=dc, par=par: mark(
                                    h.tensor_tensor(out=cv_t[:, par, dc, :],
                                                    in0=bank(dc),
                                                    in1=cv_t[:, par, dc, :],
                                                    op=OP.add)))
                    if cf == 0:
                        if g >= NG - 2:
                            fg = g - (NG - 2)
                            cxl_q[(1, fg)] = "D"
                            m[f"cxl1_{fg}"] = em.step(
                                "D", waits=[("T", m[f"kk0_{g}_15"])],
                                body=lambda h, mark, fg=fg: mark(
                                    h.dma_start(
                                        out=xn2_t[:, fg % 2],
                                        in_=xn2_sp[:, :,
                                                   fg * T:(fg + 1) * T]), 16))
                        m[f"csp{cf}_{g}"] = em.step(
                            "D", waits=[("V", m[f"cvev{cf}_{g}_7"])],
                            body=lambda h, mark, g=g, par=par: mark(
                                h.dma_start(out=cv_sp[:, :, g * T:(g + 1) * T],
                                            in_=cv_t[:, par]), 16))
                        if g + 2 < NG:
                            c_loads(cf, g + 2)
                        continue

                    for rc in range(8):
                        def t_rr(h, mark, rc=rc, cf=cf, g=g, par=par):
                            ins = None
                            for dc in range(8):
                                ins = h.matmul(bank(rc),
                                               wcb[:, dc,
                                                   rc * 128:(rc + 1) * 128],
                                               xn2_t[:, par, dc, :],
                                               start=(dc == 0), stop=(dc == 7))
                            m[f"rr{cf}_{g}_{rc}"] = mark(ins)
                        em.step("T", waits=[("V", m[f"cvt{cf}_{g}_{rc}"]),
                                            ("G", m[f"gwc{cf}"])], body=t_rr)
                        w = [("T", m[f"rr{cf}_{g}_{rc}"])]
                        if rc > 0:
                            w.append(("V", m[f"oev{cf}_{g}_{rc-1}"]))
                        elif g > 0:
                            w.append(("V", m[f"oev{cf}_{g-1}_7"]))
                        m[f"rr2{cf}_{g}_{rc}"] = em.step(
                            "S", waits=w,
                            body=lambda h, mark, rc=rc: mark(
                                h.activation(out=rr2_t[:], in_=bank(rc),
                                             func=AF.Sigmoid)))
                        def v_out(h, mark, rc=rc, cf=cf, g=g, par=par):
                            h.tensor_tensor(out=tc_t[:],
                                            in0=cv_t[:, par, rc, :],
                                            in1=rr2_t[:], op=OP.mult)
                            m[f"oev{cf}_{g}_{rc}"] = mark(
                                h.tensor_tensor(out=accC_t[:, par, rc, :],
                                                in0=accC_t[:, par, rc, :],
                                                in1=tc_t[:], op=OP.add))
                        em.step("V", waits=[("S", m[f"rr2{cf}_{g}_{rc}"])],
                                body=v_out)
                    if g == NG - 1:
                        em.step("D", waits=[("V", m[f"oev{cf}_{g}_3"])],
                                body=lambda h, mark, g=g, par=par: mark(
                                    h.dma_start(
                                        out=out_h[:, 0:4, g * T:(g + 1) * T],
                                        in_=accC_t[:, par, 0:4, :]), 16))

                        def d_osp2(h, mark, g=g, par=par):
                            mark(h.dma_start(
                                out=out_h[:, 4:8, g * T:(g + 1) * T],
                                in_=accC_t[:, par, 4:8, :]), 16)
                        m[f"osp{g}"] = em.step(
                            "D", waits=[("V", m[f"oev{cf}_{g}_7"])],
                            body=d_osp2)
                    else:
                        m[f"osp{g}"] = em.step(
                            "D", waits=[("V", m[f"oev{cf}_{g}_7"])],
                            body=lambda h, mark, g=g, par=par: mark(
                                h.dma_start(out=out_h[:, :, g * T:(g + 1) * T],
                                            in_=accC_t[:, par]), 16))
                    if g + 2 < NG:
                        c_loads(cf, g + 2)

        em.step("D", waits=[("D", em.cnt["D"])])
        em.finalize()

    return nc


def _pm(a):
    a = np.ascontiguousarray(a)
    R, C = a.shape
    return np.ascontiguousarray(a.reshape(R // 128, 128, C).transpose(1, 0, 2))


def _vec(v):
    return np.ascontiguousarray(np.asarray(v, np.float32).reshape(8, 128).T)


_NC_CACHE = {}


def _get_nc(key=(8, ("A", "F", "C"), False)):
    if key not in _NC_CACHE:
        _NC_CACHE[key] = _build(NG=key[0], phases=key[1], debug=key[2])
    return _NC_CACHE[key]


def make_in_maps(inputs):
    x = np.asarray(inputs["L0_attn_x"], np.float32)
    shared = {
        "wr": _pm(np.asarray(inputs["L0_attn_Wr"], np.float32).T),
        "ww": _pm(np.asarray(inputs["L0_attn_Ww"], np.float32).T),
        "wk": _pm(np.asarray(inputs["L0_attn_Wk"], np.float32).T),
        "wv": _pm(np.asarray(inputs["L0_attn_Wv"], np.float32).T),
        "wo": _pm(np.asarray(inputs["L0_attn_Wo"], np.float32).T),
        "wrec": _pm(np.asarray(inputs["L0_attn_Wrec"], np.float32).T),
        "wkey": _pm(np.asarray(inputs["L0_attn_Wkey"], np.float32).T),
        "wval": _pm(np.asarray(inputs["L0_attn_Wval"], np.float32).T),
        "wg": _pm(np.asarray(inputs["L0_ffn_w_gate"], np.int32).astype(np.int8).T),
        "wu": _pm(np.asarray(inputs["L0_ffn_w_up"], np.int32).astype(np.int8).T),
        "wd": _pm(np.asarray(inputs["L0_ffn_w_down"], np.int32).astype(np.int8).T),
        "g1": _vec(inputs["L0_attn_ln1_g"]),
        "b1": _vec(inputs["L0_attn_ln1_b"]),
        "g2": _vec(inputs["L0_attn_ln2_g"]),
        "b2": _vec(inputs["L0_attn_ln2_b"]),
        "td": _vec(inputs["L0_attn_time_decay"]),
    }
    in_maps = []
    for b in range(8):
        mp = dict(shared)
        mp["x"] = _pm(np.ascontiguousarray(x[b].T))
        in_maps.append(mp)
    return in_maps


def kernel(**inputs):
    in_maps = make_in_maps(inputs)
    nc = _get_nc()
    res = run_bass_kernel_spmd(nc, in_maps, core_ids=list(range(8)))
    outs = []
    for b in range(8):
        o = np.asarray(res.results[b]["out"])
        outs.append(o.transpose(1, 0, 2).reshape(1024, 4096).T)
    return np.ascontiguousarray(np.stack(outs, axis=0)).astype(np.float32)
